# version 11
# baseline (speedup 1.0000x reference)
import numpy as np
import ml_dtypes

import concourse.bass as bass
import concourse.mybir as mybir
from concourse.bass_utils import run_bass_kernel_spmd

F32 = mybir.dt.float32
BF16 = mybir.dt.bfloat16
F16 = mybir.dt.float16
F8 = mybir.dt.float8e4
U8 = mybir.dt.uint8
AF = mybir.ActivationFunctionType
ALU = mybir.AluOpType
AX = mybir.AxisListType
PM_DR = mybir.MatmulPerfMode.DoubleRow

FP8_MAX = 448.0
EPS_ATTN = 1e-6
EPS_LN = 1e-5

S_XMAXL, S_WFMAX, S_WPMAX, S_HMAXL = 0, 1, 2, 3
S_XMAXG, S_HMAXG = 4, 5
S_SX, S_SW, S_SP, S_SH = 6, 7, 8, 9
S_RSX2, S_RSW2, S_RSP2, S_DQ1, S_SQS, S_DQ2 = 10, 11, 12, 13, 14, 15
NSC = 16

ENGS = ("gpsimd", "sync", "tensor", "scalar", "vector")


def build_nc(D, N, H, F, n_cores, debug=False):
    KC = D // 128
    MCH = D // 128
    FC = F // 128
    TB = N // 128
    NTW = min(512, N)
    NT = N // NTW
    assert H == KC and FC == 4 * KC

    nc = bass.Bass("TRN2", target_bir_lowering=False,
                   detect_race_conditions=False, dynamic_dma_scratch_size=16384)

    xT = nc.dram_tensor("xT", [KC, 128, N], F32, kind="ExternalInput").ap()
    wfsc = nc.dram_tensor("wfsc", [FC // n_cores, 128, KC * 128], F32,
                          kind="ExternalInput").ap()
    wpsc = nc.dram_tensor("wpsc", [MCH // n_cores, 128, FC * 128], F32,
                          kind="ExternalInput").ap()
    wqT = nc.dram_tensor("wqT", [MCH, 128, KC * 128], F32, kind="ExternalInput").ap()
    wkT = nc.dram_tensor("wkT", [MCH, 128, KC * 128], F32, kind="ExternalInput").ap()
    wvT = nc.dram_tensor("wvT", [MCH, 128, KC * 128], F32, kind="ExternalInput").ap()
    woT = nc.dram_tensor("woT", [MCH, 128, KC * 128], F32, kind="ExternalInput").ap()
    wfcT = nc.dram_tensor("wfcT", [FC, 128, KC * 128], F32, kind="ExternalInput").ap()
    wprT = nc.dram_tensor("wprT", [MCH, 128, FC * 128], F32, kind="ExternalInput").ap()
    bq_d = nc.dram_tensor("bq_c", [128, MCH], F32, kind="ExternalInput").ap()
    bk_d = nc.dram_tensor("bk_c", [128, MCH], F32, kind="ExternalInput").ap()
    bv_d = nc.dram_tensor("bv_c", [128, MCH], F32, kind="ExternalInput").ap()
    bo_d = nc.dram_tensor("bo_c", [128, MCH], F32, kind="ExternalInput").ap()
    identf = nc.dram_tensor("identf", [128, 128], F32, kind="ExternalInput").ap()
    identb = nc.dram_tensor("identb", [128, 128], BF16, kind="ExternalInput").ap()
    decrow_d = nc.dram_tensor("decrow_d", [1, N], F32, kind="ExternalInput").ap()
    deccol_d = nc.dram_tensor("deccol_d", [128, TB], F32, kind="ExternalInput").ap()
    outT = nc.dram_tensor("outT", [KC, 128, N], F32, kind="ExternalOutput").ap()

    dbg = dict(kind="ExternalOutput") if debug else {}
    h_sp = nc.dram_tensor("h_sp", [FC, 128, N], F16, **dbg).ap()
    x1_sp = nc.dram_tensor("x1_sp", [KC, 128, N], BF16, **dbg).ap()
    o_sp = nc.dram_tensor("o_sp", [KC, 128, N], F16, **dbg).ap()
    sc_dram = nc.dram_tensor("sc_dram", [1, NSC], F32, **dbg).ap()
    cc_in1 = nc.dram_tensor("cc_in1", [1, 16], F32)
    cc_out1 = nc.dram_tensor("cc_out1", [1, 16], F32)
    cc_in2 = nc.dram_tensor("cc_in2", [1, 16], F32)
    cc_out2 = nc.dram_tensor("cc_out2", [1, 16], F32)

    from contextlib import ExitStack
    es = ExitStack()
    sb = lambda name, shape, dt: es.enter_context(nc.sbuf_tensor(name, shape, dt))
    psm = lambda name, shape: es.enter_context(nc.psum_tensor(name, shape, F32))
    mksem = lambda name: es.enter_context(nc.semaphore(name))

    mega = sb("mega", [128, 4 * KC * N], U8)
    bigA = mega[:, 0:KC * N * 2].bitcast(BF16)
    bigA_h = mega[:, 0:KC * N * 2].bitcast(F16)
    bigB = mega[:, KC * N * 2:4 * KC * N].bitcast(BF16)
    x8v = mega[:, KC * N * 2:KC * N * 3].bitcast(F8)
    h8v = mega[:, :].bitcast(F8)

    M2B = 48 * 1024
    mega2 = sb("mega2", [128, M2B], U8)

    def m2(lo, nbytes, dt):
        assert lo + nbytes <= M2B, (lo, nbytes)
        return mega2[:, lo:lo + nbytes].bitcast(dt)

    wbuf = [m2(0, KC * 256, BF16), m2(KC * 256, KC * 256, BF16)]
    _o2 = KC * 512
    kfm = m2(_o2, 2 * N, BF16)
    vfm = m2(_o2 + 2 * N, 2 * N, BF16)
    _o3 = _o2 + 4 * N
    ktok = [m2(_o3 + i * 512, 256, BF16) for i in range(4)]
    vtok = [m2(_o3 + i * 512 + 256, 256, BF16) for i in range(4)]
    kvsb = m2(_o3 + 2048, 256, BF16)
    ksumc = m2(_o3 + 2304, 2, BF16)
    _o4 = _o3 + 2560
    zbsb = [m2(_o4, NTW * 4, F32), m2(_o4 + NTW * 4, NTW * 4, F32)]
    _o5 = _o4 + NTW * 8
    pre = [m2(_o5, 4096, F32), m2(_o5 + 4096, 4096, F32)]
    _o6 = _o5 + 8192
    rowC = m2(_o6, 4 * N, F32)[0:1, :]
    rowD = m2(_o6 + 4 * N, 4 * N, F32)[0:1, :]
    assert _o6 + 8 * N <= M2B
    wfbuf = [m2(0, KC * 512, F32), m2(KC * 512, KC * 512, F32)]
    _f1 = KC * 1024
    wf8b = [m2(_f1, KC * 128, F8), m2(_f1 + KC * 128, KC * 128, F8)]
    _f2 = _f1 + 2 * KC * 128
    hbuf = [m2(_f2, 2 * N, F16), m2(_f2 + 2 * N, 2 * N, F16)]
    wpbuf = [m2(0, 8192, F32), m2(8192, 8192, F32)]
    wp8b = [m2(16384, FC * 128, F8), m2(16384 + FC * 128, FC * 128, F8)]
    _m2o = 16384 + 2 * FC * 128
    obuf = [m2(_m2o, 2 * N, F16), m2(_m2o + 2 * N, 2 * N, F16)]
    outb = [m2(0, 4 * N, F32), m2(4 * N, 4 * N, F32)]
    x1b = [m2(8 * N, 2 * N, BF16), m2(10 * N, 2 * N, BF16)]
    _sq = M2B - 4 * N
    sqb = [m2(_sq, 2 * N, BF16), m2(_sq + 2 * N, 2 * N, BF16)]
    sqh = [m2(_sq, 2 * N, F16), m2(_sq + 2 * N, 2 * N, F16)]
    assert _sq >= 12 * N and _sq >= _m2o + 4 * N and _sq >= _f2 + 4 * N
    assert _sq >= _o6 + 4 * N

    xb = [sb("xb0", [128, N], BF16), sb("xb1", [128, N], BF16)]
    xbh = [xb[0][:, :].bitcast(F16), xb[1][:, :].bitcast(F16)]
    t32a = sb("t32a", [128, NTW], F32)
    t32b = sb("t32b", [128, NTW], F32)
    rowA = sb("rowA", [1, N], F32)
    rowB = sb("rowB", [1, N], F32)
    idf = sb("idf", [128, 128], F32)
    idb = sb("idb", [128, 128], BF16)
    onesrow = sb("onesrow", [1, 128], F32)
    ones1f = sb("ones1f", [1, 1], F32)
    onescol = sb("onescol", [128, 1], BF16)
    onescol_h = sb("onescol_h", [128, 1], F16)
    sc = sb("sc", [1, NSC], F32)
    scb = sb("scb", [128, NSC], F32)
    wfam = sb("wfam", [128, 128], F32)
    wpam = sb("wpam", [128, 128], F32)
    xam = sb("xam", [128, 64], F32)
    ham = sb("ham", [128, 64], F32)
    amaxf = [sb(f"amaxf{i}", [128, 1], F32) for i in range(4)]
    prem = [sb("prem0", [128, 1], F32), sb("prem1", [128, 1], F32)]
    vsink = sb("vsink", [128, 1], F32)
    asink = sb("asink", [128, 1], F32)
    biasq = sb("biasq", [128, MCH], F32)
    biask = sb("biask", [128, MCH], F32)
    biasv = sb("biasv", [128, MCH], F32)
    biaso = sb("biaso", [128, MCH], F32)
    deccol = sb("deccol", [128, TB], F32)

    pb = [psm(f"pb{j}", [128, 512]) for j in range(8)]
    pbb = {j: pb[j][:, :].bitcast(BF16) for j in (0, 1, 6, 7)}

    s_gld = mksem("s_gld")
    s_te = mksem("s_te")
    s_dve = mksem("s_dve")
    s_act = mksem("s_act")
    s_cc = mksem("s_cc")
    sconst = mksem("sconst")
    sxb = [mksem("sxb0"), mksem("sxb1")]
    sxg = [mksem("sxg0"), mksem("sxg1")]
    swb = [mksem("swb0"), mksem("swb1")]
    swf = [mksem("swf0"), mksem("swf1")]
    swp = [mksem("swp0"), mksem("swp1")]
    spre = [mksem("spre0"), mksem("spre1")]
    sx1b = [mksem("sx1b0"), mksem("sx1b1")]
    sph = [mksem("sph0"), mksem("sph1")]
    spo = [mksem("spo0"), mksem("spo1")]
    spx = mksem("spx")
    sout = [mksem("sout0"), mksem("sout1")]
    ALLSEMS = [s_gld, s_te, s_dve, s_act, s_cc, sconst, spx] + \
        sxb + sxg + swb + swf + swp + spre + sx1b + sph + spo + sout

    recs = {e: [] for e in ENGS}
    CNT = {s.name: 0 for s in ALLSEMS}
    EV = {}
    READY = {}
    FREE = {}

    def emit(eng, waits, build, sem=None, n=1, key=None):
        wl = []
        seen = set()
        for w in waits:
            if w is None:
                continue
            s, v = EV[w] if isinstance(w, str) else w
            if (s.name, v) in seen:
                continue
            seen.add((s.name, v))
            wl.append((s, v))
        if sem is not None:
            CNT[sem.name] += n
            val = CNT[sem.name]
            if key is not None:
                EV[key] = (sem, val)
            recs[eng].append((wl, build, sem, n))
            return key
        recs[eng].append((wl, build, None, 0))
        return None

    def produce(res, key):
        READY[res] = key

    def ready(res):
        return READY.get(res)

    def consumed(res, key):
        FREE[res] = [key]

    def consumed_add(res, key):
        FREE.setdefault(res, []).append(key)

    def free_waits(res):
        w = FREE.get(res, [])
        FREE[res] = []
        return w

    def sl(base, nt):
        return slice(base + nt * NTW, base + (nt + 1) * NTW)

    DBG_BUFS = {}

    def dbg_dump(tag, src_ap, shape, dt, wait_key):
        if debug != 2:
            return
        t = nc.dram_tensor(f"d_{tag}", shape, dt, kind="ExternalOutput").ap()
        DBG_BUFS[tag] = t
        key = f"dbg_{tag}"
        emit("gpsimd", [wait_key],
             lambda e: e.dma_start(out=t, in_=src_ap, allow_slow_non_contiguous=True),
             s_gld, 16, key=key)
        return key

    const_pairs = [(idf[:, :], identf), (idb[:, :], identb), (biasq[:, :], bq_d),
                   (biask[:, :], bk_d), (biasv[:, :], bv_d), (biaso[:, :], bo_d),
                   (deccol[:, 0:TB], deccol_d)]
    for i, (dst, src) in enumerate(const_pairs):
        emit("gpsimd", [], lambda e, dst=dst, src=src: e.dma_start(out=dst, in_=src),
             sconst, 16, key=f"c{i}")
    EV["consts"] = EV[f"c{len(const_pairs)-1}"]
    emit("vector", [], lambda e: e.memset(onesrow[:, :], 1.0))
    emit("vector", [], lambda e: e.memset(ones1f[:, :], 1.0))
    emit("vector", [], lambda e: e.memset(onescol[:, :], 1.0))
    emit("vector", [], lambda e: e.memset(onescol_h[:, :], 1.0))
    emit("vector", [], lambda e: e.memset(sc[:, :], 0.0))
    emit("vector", [], lambda e: e.memset(wfam[:, :], 0.0))
    emit("vector", [], lambda e: e.memset(wpam[:, :], 0.0))
    emit("vector", [], lambda e: e.memset(xam[:, :], 0.0))
    emit("vector", [], lambda e: e.memset(ham[:, :], 0.0))
    emit("vector", [], lambda e: e.memset(scb[:, :], 0.0), s_dve, 1, key="ones_done")
    for _e in ("sync", "tensor", "scalar", "vector"):
        emit(_e, ["consts", "ones_done"], lambda e: e.nop())

    def ln_stats(tag, kind, src_ready_key=None, src_key_fn=None):
        f16src = kind == "osp"
        oc = onescol_h if f16src else onescol
        for k in range(KC):
            pp = k % 2
            if kind == "x0":
                emit("gpsimd", free_waits(f"xb{pp}"),
                     lambda e, k=k, pp=pp: e.dma_start(out=xb[pp][:, :], in_=xT[k]),
                     sxg[pp], 16, key=f"{tag}_ld{k}")
                srcap = xb[pp][:, :]
                rk = f"{tag}_ld{k}"
            elif kind == "osp":
                emit("sync", [f"osp{k}"] + free_waits(f"xb{pp}"),
                     lambda e, k=k, pp=pp: e.dma_start(out=xbh[pp], in_=o_sp[k]),
                     sxb[pp], 16, key=f"{tag}_ld{k}")
                srcap = xbh[pp]
                rk = f"{tag}_ld{k}"
            else:
                srcap = (bigA if kind == "bigA" else bigB)[:, k * N:(k + 1) * N]
                rk = src_key_fn(k) if src_key_fn else src_ready_key
            sq = sqh[pp] if f16src else sqb[pp]
            hw = N // 2
            sqfree = free_waits(f"sq{pp}")
            emit("scalar", [rk] + sqfree,
                 lambda e, srcap=srcap, sq=sq, hw=hw:
                     e.activation(sq[:, 0:hw], srcap[:, 0:hw], AF.Square),
                 s_act, 1, key=f"{tag}_sqa{k}")
            if f16src:
                def sqv(e, srcap=srcap, sq=sq, hw=hw):
                    return e.tensor_mul(sq[:, hw:N], srcap[:, hw:N], srcap[:, hw:N])
            else:
                def sqv(e, srcap=srcap, sq=sq, hw=hw):
                    return e.tensor_mul(sq[:, hw:N], srcap[:, hw:N], srcap[:, hw:N])
            emit("vector", [rk] + sqfree, sqv, s_dve, 1, key=f"{tag}_sqb{k}")
            produce(f"sq{pp}", f"{tag}_sqa{k}")
            tw = [rk, f"{tag}_sqa{k}", f"{tag}_sqb{k}", "ones_done"]
            if k == 0:
                for j in range(8):
                    tw += free_waits(f"pb{j}")

            def mmfn(e, k=k, srcap=srcap, sq=sq, oc=oc):
                last = None
                for nt in range(NT):
                    e.matmul(pb[nt][0:1, 0:NTW], oc[:, :],
                             srcap[:, nt * NTW:(nt + 1) * NTW],
                             start=(k == 0), stop=(k == KC - 1))
                    last = e.matmul(pb[4 + nt][0:1, 0:NTW], oc[:, :],
                                    sq[:, nt * NTW:(nt + 1) * NTW],
                                    start=(k == 0), stop=(k == KC - 1))
                return last
            emit("tensor", tw, mmfn, s_te, 1, key=f"{tag}_mm{k}")
            consumed(f"sq{pp}", f"{tag}_mm{k}")
            if kind in ("x0", "osp"):
                consumed(f"xb{pp}", f"{tag}_mm{k}")

        def rowsfn(e):
            for nt in range(NT):
                e.tensor_copy(rowA[0:1, nt * NTW:(nt + 1) * NTW], pb[nt][0:1, 0:NTW])
                e.tensor_copy(rowB[0:1, nt * NTW:(nt + 1) * NTW], pb[4 + nt][0:1, 0:NTW])
            e.tensor_scalar_mul(rowA[:, :], rowA[:, :], 1.0 / D)
            e.tensor_scalar_mul(rowB[:, :], rowB[:, :], 1.0 / D)
            e.tensor_mul(rowC, rowA[:, :], rowA[:, :])
            e.tensor_sub(rowB[:, :], rowB[:, :], rowC)
            e.tensor_scalar_add(rowB[:, :], rowB[:, :], EPS_LN)
            return e.tensor_copy(vsink[0:1, :], rowB[0:1, 0:1])
        emit("vector", [f"{tag}_mm{KC-1}"], rowsfn, s_dve, 1, key=f"{tag}_var")
        for j in range(8):
            consumed(f"pb{j}", f"{tag}_var")
        def sqrtfn(e):
            e.activation(rowB[:, :], rowB[:, :], AF.Sqrt)
            return e.copy(asink[0:1, :], rowB[0:1, 0:1])
        emit("scalar", [f"{tag}_var"], sqrtfn, s_act, 1, key=f"{tag}_sqrt")
        def recfn(e):
            e.reciprocal(rowB[:, :], rowB[:, :])
            return e.tensor_copy(vsink[0:1, :], rowB[0:1, 0:1])
        emit("vector", [f"{tag}_sqrt"], recfn, s_dve, 1, key=f"{tag}_rows")

    def ln_bcasts(tag):
        w = [f"{tag}_rows"]
        for j in range(8):
            w += free_waits(f"pb{j}")

        def bfn(e):
            last = None
            for nt in range(NT):
                e.matmul(pb[nt][:, 0:NTW], onesrow[:, :],
                         rowA[0:1, nt * NTW:(nt + 1) * NTW], start=True, stop=True)
                last = e.matmul(pb[4 + nt][:, 0:NTW], onesrow[:, :],
                                rowB[0:1, nt * NTW:(nt + 1) * NTW], start=True, stop=True)
            return last
        emit("tensor", w, bfn, s_te, 1, key=f"{tag}_bc")

    ln_stats("ln0", "x0")
    emit("gpsimd", [f"ln0_mm{KC-1}"], lambda e: e.dma_start(out=rowD, in_=decrow_d),
         sconst, 16, key="ld_decrow")
    ln_bcasts("ln0")
    for k in range(KC):
        pp = k % 2
        emit("gpsimd", free_waits(f"xb{pp}"),
             lambda e, k=k, pp=pp: e.dma_start(out=xb[pp][:, :], in_=xT[k]),
             sxg[pp], 16, key=f"ln0a_ld{k}")

        def apfn(e, k=k, pp=pp):
            last = None
            for nt in range(NT):
                e.tensor_tensor(t32a[:, 0:NTW], xb[pp][:, nt * NTW:(nt + 1) * NTW],
                                pb[nt][:, 0:NTW], ALU.subtract)
                last = e.tensor_tensor(bigA[:, sl(k * N, nt)], t32a[:, 0:NTW],
                                       pb[4 + nt][:, 0:NTW], ALU.mult)
            return last
        emit("vector", [f"ln0a_ld{k}", "ln0_bc"], apfn, s_dve, 1, key=f"ln0a_k{k}")
        consumed(f"xb{pp}", f"ln0a_k{k}")
    EV["ln0_done"] = EV[f"ln0a_k{KC-1}"]
    for j in range(8):
        consumed(f"pb{j}", f"ln0a_k{KC-1}")

    def prepass():
        sub = 1024
        items = [("prf", wfsc, FC // n_cores, (KC * 128) // sub, wfam),
                 ("prp", wpsc, MCH // n_cores, (FC * 128) // sub, wpam)]
        idx = 0
        for tg, src, M, nsub, acc in items:
            assert M * nsub <= 64
            for i in range(M * nsub):
                m, j = divmod(i, nsub)
                pp = idx % 2
                idx += 1
                emit("sync", free_waits(f"pre{pp}"),
                     lambda e, m=m, j=j, pp=pp, src=src:
                         e.dma_start(out=pre[pp][:, 0:sub], in_=src[m][:, j * sub:(j + 1) * sub]),
                     spre[pp], 16, key=f"{tg}_ld{i}")

                def rfn(e, i=i, pp=pp, acc=acc):
                    return e.reduce_max(acc[:, i:i + 1], pre[pp][:, 0:sub], axis=AX.X,
                                        apply_absolute_value=True)
                emit("vector", [f"{tg}_ld{i}"], rfn, s_dve, 1, key=f"{tg}_{i}")
                consumed(f"pre{pp}", f"{tg}_{i}")
        EV["prepass"] = EV[f"prp_{(MCH // n_cores) * ((FC * 128) // sub) - 1}"]

    prepass()


    for m in range(MCH):
        pp = m % 2
        emit("gpsimd", free_waits(f"wbuf{pp}"),
             lambda e, m=m, pp=pp: e.dma_start(out=wbuf[pp], in_=wqT[m]),
             swb[pp], 16, key=f"qw{m}")
        base = 0 if pp == 0 else 4

        def qmm_k(e, k, pp, base):
            last = None
            for nt in range(NT):
                last = e.matmul(pb[base + nt][:, 0:NTW], wbuf[pp][:, 128 * k:128 * (k + 1)],
                                bigA[:, sl(k * N, nt)], start=(k == 0), stop=(k == KC - 1))
            return last

        if m == 0:
            pbfree = []
            for nt in range(NT):
                pbfree += free_waits(f"pb{base + nt}")
            for k in range(KC):
                tw = [f"qw{m}", f"ln0a_k{k}"] + (pbfree if k == 0 else [])
                emit("tensor", tw, lambda e, k=k, pp=pp, base=base: qmm_k(e, k, pp, base),
                     s_te, 1, key=f"qmm{m}_k{k}")
            EV[f"qmm{m}"] = EV[f"qmm{m}_k{KC-1}"]
        else:
            tw = [f"qw{m}", "ln0_done"]
            for nt in range(NT):
                tw += free_waits(f"pb{base + nt}")

            def qmm(e, m=m, pp=pp, base=base):
                last = None
                for k in range(KC):
                    last = qmm_k(e, k, pp, base)
                return last
            emit("tensor", tw, qmm, s_te, 1, key=f"qmm{m}")
        consumed(f"wbuf{pp}", f"qmm{m}")

        def qev(e, m=m, base=base):
            last = None
            for nt in range(NT):
                last = e.activation(bigB[:, sl(m * N, nt)], pb[base + nt][:, 0:NTW],
                                    AF.Relu, bias=biasq[:, m:m + 1], scale=1.0)
            return last
        emit("scalar", [f"qmm{m}"], qev, s_act, 1, key=f"qev{m}")
        for nt in range(NT):
            consumed(f"pb{base + nt}", f"qev{m}")
    EV["q_done"] = EV[f"qev{MCH-1}"]
    dbg_dump("q", bigB, [128, KC * N], BF16, "q_done")

    for h in range(H):
        emit("gpsimd", free_waits("wbuf0"),
             lambda e, h=h: e.dma_start(out=wbuf[0], in_=wkT[h]), swb[0], 16, key=f"wk{h}")
        emit("gpsimd", free_waits("wbuf1"),
             lambda e, h=h: e.dma_start(out=wbuf[1], in_=wvT[h]), swb[1], 16, key=f"wv{h}")
        for which, wkey, outt, biast, func in (
                ("k", f"wk{h}", kfm, biask, AF.Relu), ("v", f"wv{h}", vfm, biasv, AF.Identity)):
            wb = wbuf[0] if which == "k" else wbuf[1]
            for nt in range(NT):
                bank = nt % 2
                tw = [wkey] + free_waits(f"pb{bank}")
                if nt == 0:
                    tw += free_waits(which + "fm")

                def pmm(e, nt=nt, bank=bank, wb=wb):
                    last = None
                    for k in range(KC):
                        last = e.matmul(pb[bank][:, 0:NTW], wb[:, 128 * k:128 * (k + 1)],
                                        bigA[:, sl(k * N, nt)], start=(k == 0), stop=(k == KC - 1))
                    return last
                emit("tensor", tw, pmm, s_te, 1, key=f"{which}mm_{h}_{nt}")
                emit("scalar", [f"{which}mm_{h}_{nt}"],
                     lambda e, nt=nt, bank=bank, outt=outt, biast=biast, func=func, h=h:
                         e.activation(outt[:, nt * NTW:(nt + 1) * NTW], pb[bank][:, 0:NTW],
                                      func, bias=biast[:, h:h + 1], scale=1.0),
                     s_act, 1, key=f"{which}ev_{h}_{nt}")
                consumed(f"pb{bank}", f"{which}ev_{h}_{nt}")
            consumed(f"wbuf{0 if which == 'k' else 1}", f"{which}mm_{h}_{NT-1}")
        for tb in range(TB):
            bnk = (0, 1, 6, 7)[tb % 4]
            sl4 = tb % 4
            tw = [f"kev_{h}_{NT-1}", f"vev_{h}_{NT-1}"] + free_waits(f"pb{bnk}")
            if tb == 0:
                tw += free_waits("pb2")

            def trfn(e, tb=tb, bnk=bnk):
                e.transpose(pbb[bnk][:, 0:128], kfm[:, 128 * tb:128 * (tb + 1)], idb[:, :])
                return e.transpose(pbb[bnk][:, 128:256], vfm[:, 128 * tb:128 * (tb + 1)], idb[:, :])
            emit("tensor", tw, trfn, s_te, 1, key=f"tr_{h}_{tb}")

            def tokfn(e, tb=tb, bnk=bnk, sl4=sl4):
                e.tensor_scalar_mul(ktok[sl4][:, :], pbb[bnk][:, 0:128], deccol[:, tb:tb + 1])
                return e.tensor_copy(vtok[sl4][:, :], pbb[bnk][:, 128:256])
            emit("vector", [f"tr_{h}_{tb}", "consts"] + free_waits(f"ktok{sl4}"),
                 tokfn, s_dve, 1, key=f"tok_{h}_{tb}")
            consumed(f"pb{bnk}", f"tok_{h}_{tb}")

            def kvfn(e, tb=tb, sl4=sl4):
                e.matmul(pb[2][:, 0:128], ktok[sl4][:, :], vtok[sl4][:, :],
                         start=(tb == 0), stop=False)
                return e.matmul(pb[2][:, 128:129], ktok[sl4][:, :], onescol[:, :],
                                start=False, stop=(tb == TB - 1))
            emit("tensor", [f"tok_{h}_{tb}"], kvfn, s_te, 1,
                 key=f"kvmm_{h}_{tb}")
            consumed(f"ktok{sl4}", f"kvmm_{h}_{tb}")
        consumed("kfm", f"tr_{h}_{TB-1}")
        consumed("vfm", f"tr_{h}_{TB-1}")
        if h == 0:
            kk = dbg_dump("kfm0", kfm, [128, N], BF16, f"kev_0_{NT-1}")
            if kk:
                consumed_add("kfm", kk)


        def kvev(e):
            e.tensor_copy(kvsb[:, :], pb[2][:, 0:128])
            return e.tensor_copy(ksumc[:, :], pb[2][:, 128:129])
        emit("vector", [f"kvmm_{h}_{TB-1}"], kvev, s_dve, 1, key=f"kvev_{h}")
        consumed("pb2", f"kvev_{h}")
        if h == 0:
            dbg_dump("kv", kvsb[:, :], [128, 128], BF16, "kvev_0")
            dbg_dump("kfm", kfm, [128, N], BF16, "kvev_0")
        for nt in range(NT):
            emit("tensor", [f"kvev_{h}"] + free_waits("pb3"),
                 lambda e, h=h, nt=nt: e.matmul(pb[3][0:1, 0:NTW], ksumc[:, :],
                                                bigB[:, sl(h * N, nt)], start=True, stop=True),
                 s_te, 1, key=f"zmm_{h}_{nt}")
            emit("vector", [f"zmm_{h}_{nt}"],
                 lambda e, nt=nt: e.tensor_copy(rowC[0:1, nt * NTW:(nt + 1) * NTW],
                                                pb[3][0:1, 0:NTW]),
                 s_dve, 1, key=f"zev_{h}_{nt}")
            consumed("pb3", f"zev_{h}_{nt}")

        def zrfn(e):
            e.tensor_mul(rowC, rowC, rowD)
            e.tensor_scalar_add(rowC, rowC, EPS_ATTN)
            e.reciprocal(rowC, rowC)
            e.tensor_mul(rowC, rowC, rowD)
            return e.tensor_copy(vsink[0:1, :], rowC[0:1, 0:1])
        emit("vector", [f"zev_{h}_{NT-1}", "ld_decrow" if h == 0 else None],
             zrfn, s_dve, 1, key=f"zr_{h}")
        if h == 0:
            kz = dbg_dump("zr", rowC, [1, N], F32, "zr_0")
            if kz:
                for _nt in range(NT):
                    FREE.setdefault(f"pb{6 + _nt % 2}", []).append(kz)
        for nt in range(NT):
            pn = 4 + nt % 2
            pz = 6 + nt % 2
            emit("tensor", [f"kvev_{h}"] + free_waits(f"pb{pn}"),
                 lambda e, h=h, nt=nt, pn=pn: e.matmul(pb[pn][:, 0:NTW], kvsb[:, :],
                                                       bigB[:, sl(h * N, nt)],
                                                       start=True, stop=True),
                 s_te, 1, key=f"nmm_{h}_{nt}")
            emit("tensor", [f"zr_{h}"] + free_waits(f"pb{pz}"),
                 lambda e, nt=nt, pz=pz: e.matmul(pb[pz][:, 0:NTW], onesrow[:, :],
                                                  rowC[0:1, nt * NTW:(nt + 1) * NTW],
                                                  start=True, stop=True),
                 s_te, 1, key=f"zbmm_{h}_{nt}")
            emit("scalar", [f"zbmm_{h}_{nt}"] + free_waits(f"zbsb{nt % 2}"),
                 lambda e, nt=nt, pz=pz: e.activation(zbsb[nt % 2][:, 0:NTW],
                                                      pb[pz][:, 0:NTW], AF.Identity),
                 s_act, 1, key=f"zbev_{h}_{nt}")
            consumed(f"pb{pz}", f"zbev_{h}_{nt}")
            emit("vector", [f"nmm_{h}_{nt}", f"zbev_{h}_{nt}"],
                 lambda e, h=h, nt=nt, pn=pn: e.tensor_tensor(bigB[:, sl(h * N, nt)],
                                                              zbsb[nt % 2][:, 0:NTW],
                                                              pb[pn][:, 0:NTW], ALU.mult),
                 s_dve, 1, key=f"pre_{h}_{nt}")
            consumed(f"pb{pn}", f"pre_{h}_{nt}")
            consumed(f"zbsb{nt % 2}", f"pre_{h}_{nt}")
    EV["attn_done"] = EV[f"pre_{H-1}_{NT-1}"]
    k0 = dbg_dump("pre", bigB, [128, KC * N], BF16, "attn_done")
    if k0:
        FREE.setdefault("pb0", []).append(k0)

    for m in range(MCH):
        pp = m % 2
        emit("gpsimd", free_waits(f"wbuf{pp}"),
             lambda e, m=m, pp=pp: e.dma_start(out=wbuf[pp], in_=woT[m]),
             swb[pp], 16, key=f"ow{m}")
        base = 0 if pp == 0 else 4
        tw = [f"ow{m}", "attn_done"]
        for nt in range(NT):
            tw += free_waits(f"pb{base + nt}")

        def omm(e, m=m, pp=pp, base=base):
            last = None
            for k in range(KC):
                for nt in range(NT):
                    last = e.matmul(pb[base + nt][:, 0:NTW], wbuf[pp][:, 128 * k:128 * (k + 1)],
                                    bigB[:, sl(k * N, nt)], start=(k == 0), stop=(k == KC - 1))
            return last
        emit("tensor", tw, omm, s_te, 1, key=f"omm{m}")
        consumed(f"wbuf{pp}", f"omm{m}")

        def oev(e, m=m, base=base):
            last = None
            for nt in range(NT):
                last = e.activation(bigA[:, sl(m * N, nt)], pb[base + nt][:, 0:NTW],
                                    AF.Identity, bias=biaso[:, m:m + 1], scale=1.0)
            return last
        emit("scalar", [f"omm{m}"], oev, s_act, 1, key=f"oev{m}")
        for nt in range(NT):
            consumed(f"pb{base + nt}", f"oev{m}")
    EV["a_done"] = EV[f"oev{MCH-1}"]
    ka = dbg_dump("a", bigA, [128, KC * N], BF16, "a_done")
    if ka:
        FREE.setdefault("xb0", []).append(ka)

    ln_stats("ln1", "bigA", src_key_fn=lambda k: f"oev{k}")
    dbg_dump("mu1", rowA[:, :], [1, N], F32, "ln1_rows")
    dbg_dump("r1", rowB[:, :], [1, N], F32, "ln1_rows")
    ln_bcasts("ln1")
    for k in range(KC):
        pp = k % 2
        emit("gpsimd", free_waits(f"xb{pp}"),
             lambda e, k=k, pp=pp: e.dma_start(out=xb[pp][:, :], in_=xT[k]),
             sxg[pp], 16, key=f"x1_ld{k}")

        def x1fn(e, k=k, pp=pp):
            last = None
            for nt in range(NT):
                e.tensor_tensor(t32a[:, 0:NTW], bigA[:, sl(k * N, nt)], pb[nt][:, 0:NTW],
                                ALU.subtract)
                e.tensor_tensor(t32b[:, 0:NTW], t32a[:, 0:NTW], pb[4 + nt][:, 0:NTW], ALU.mult)
                last = e.tensor_tensor(bigB[:, sl(k * N, nt)], t32b[:, 0:NTW],
                                       xb[pp][:, nt * NTW:(nt + 1) * NTW], ALU.add)
            return last
        emit("vector", [f"x1_ld{k}", "ln1_bc", f"pre_{H-1}_{NT-1}" if k == 0 else None],
             x1fn, s_dve, 1, key=f"x1_k{k}")
        consumed(f"xb{pp}", f"x1_k{k}")
        emit("scalar", [f"x1_k{k}"],
             lambda e, k=k: e.dma_start(out=x1_sp[k], in_=bigB[:, k * N:(k + 1) * N]),
             spx, 16, key=f"x1sp{k}")
    EV["x1_done"] = EV[f"x1_k{KC-1}"]
    EV["x1_spilled"] = EV[f"x1sp{KC-1}"]
    for j in range(8):
        consumed(f"pb{j}", f"x1_k{KC-1}")

    ln_stats("ln2", "bigB", src_key_fn=lambda k: f"x1_k{k}")
    ln_bcasts("ln2")
    for k in range(KC):
        def x2fn(e, k=k):
            last = None
            for nt in range(NT):
                it = k * NT + nt
                e.tensor_tensor(t32a[:, 0:NTW], bigB[:, sl(k * N, nt)], pb[nt][:, 0:NTW],
                                ALU.subtract)
                e.tensor_tensor(bigA_h[:, sl(k * N, nt)], t32a[:, 0:NTW],
                                pb[4 + nt][:, 0:NTW], ALU.mult)
                last = e.reduce_max(xam[:, it:it + 1], bigA_h[:, sl(k * N, nt)], axis=AX.X,
                                    apply_absolute_value=True)
            return last
        emit("vector", ["ln2_bc"], x2fn, s_dve, 1, key=f"x2d_k{k}")
    EV["x2d_done"] = EV[f"x2d_k{KC-1}"]
    for j in range(8):
        consumed(f"pb{j}", f"x2d_k{KC-1}")

    def amred(e):
        e.reduce_max(amaxf[2][:, :], xam[:, 0:KC * NT], axis=AX.X)
        e.reduce_max(amaxf[0][:, :], wfam[:, :], axis=AX.X)
        return e.reduce_max(amaxf[1][:, :], wpam[:, :], axis=AX.X)
    emit("vector", ["x2d_done", "prepass"], amred, s_dve, 1, key="amred")

    def tr3(e):
        e.transpose(pb[0][0:1, 0:128], amaxf[2][:, :], idf[:, :])
        e.transpose(pb[1][0:1, 0:128], amaxf[0][:, :], idf[:, :])
        return e.transpose(pb[2][0:1, 0:128], amaxf[1][:, :], idf[:, :])
    emit("tensor", ["amred", "consts"] + free_waits("pb0")
         + free_waits("pb1") + free_waits("pb2"), tr3, s_te, 1, key="amx_tr")

    def amx(e):
        e.reduce_max(sc[0:1, S_XMAXL:S_XMAXL + 1], pb[0][0:1, 0:128], axis=AX.X)
        e.reduce_max(sc[0:1, S_WFMAX:S_WFMAX + 1], pb[1][0:1, 0:128], axis=AX.X)
        return e.reduce_max(sc[0:1, S_WPMAX:S_WPMAX + 1], pb[2][0:1, 0:128], axis=AX.X)
    emit("vector", ["amx_tr"], amx, s_dve, 1, key="amx_sc")
    consumed("pb0", "amx_sc")
    consumed("pb1", "amx_sc")
    consumed("pb2", "amx_sc")

    emit("gpsimd", ["amx_sc"], lambda e: e.dma_start(out=cc_in1.ap(), in_=sc[0:1, 0:16]),
         s_gld, 16, key="cc1_out")
    emit("gpsimd", ["cc1_out"],
         lambda e: e.collective_compute("AllReduce", ALU.max,
                                        replica_groups=[list(range(n_cores))],
                                        ins=[cc_in1.ap().opt()], outs=[cc_out1.ap().opt()]),
         s_cc, 1, key="cc1")
    emit("gpsimd", ["cc1"],
         lambda e: e.dma_start(out=sc[0:1, S_XMAXG:S_XMAXG + 1],
                               in_=cc_out1.ap()[0:1, S_XMAXL:S_XMAXL + 1]),
         s_gld, 16, key="cc1_back0")
    emit("gpsimd", ["cc1_back0"],
         lambda e: e.dma_start(out=sc[0:1, S_WFMAX:S_WPMAX + 1],
                               in_=cc_out1.ap()[0:1, S_WFMAX:S_WPMAX + 1]),
         s_gld, 16, key="cc1_back")

    def sc1(e):
        e.tensor_scalar(sc[0:1, S_SX:S_SX + 1], sc[0:1, S_XMAXG:S_XMAXG + 1],
                        1.0 / FP8_MAX, 1e-6, ALU.mult, ALU.max)
        e.tensor_scalar(sc[0:1, S_SW:S_SW + 1], sc[0:1, S_WFMAX:S_WFMAX + 1],
                        1.0 / FP8_MAX, 1e-6, ALU.mult, ALU.max)
        e.tensor_scalar(sc[0:1, S_SP:S_SP + 1], sc[0:1, S_WPMAX:S_WPMAX + 1],
                        1.0 / FP8_MAX, 1e-6, ALU.mult, ALU.max)
        e.reciprocal(sc[0:1, 0:1], sc[0:1, S_SX:S_SX + 1])
        e.reciprocal(sc[0:1, 1:2], sc[0:1, S_SW:S_SW + 1])
        e.reciprocal(sc[0:1, 2:3], sc[0:1, S_SP:S_SP + 1])
        e.scalar_tensor_tensor(sc[0:1, S_DQ1:S_DQ1 + 1], sc[0:1, S_SX:S_SX + 1], 4.0,
                               sc[0:1, S_SW:S_SW + 1], ALU.mult, ALU.mult)
        e.tensor_scalar_mul(sc[0:1, S_RSX2:S_RSX2 + 1], sc[0:1, 0:1], 0.5)
        e.tensor_scalar_mul(sc[0:1, S_RSW2:S_RSW2 + 1], sc[0:1, 1:2], 0.5)
        return e.tensor_scalar_mul(sc[0:1, S_RSP2:S_RSP2 + 1], sc[0:1, 2:3], 0.5)
    emit("vector", ["cc1_back"], sc1, s_dve, 1, key="sc1_math")
    emit("tensor", ["sc1_math"] + free_waits("pb3"),
         lambda e: e.matmul(pb[3][:, 0:NSC], onesrow[:, :], sc[0:1, 0:NSC],
                            start=True, stop=True),
         s_te, 1, key="scb1_mm")
    emit("vector", ["scb1_mm"],
         lambda e: e.tensor_copy(scb[:, :], pb[3][:, 0:NSC]),
         s_dve, 1, key="scb1")
    consumed("pb3", "scb1")

    for k in range(KC):
        if k % 2 == 0:
            def x8fn(e, k=k):
                last = None
                for nt in range(NT):
                    o = nt * (KC * NTW) + k * NTW
                    last = e.activation(x8v[:, o:o + NTW], bigA_h[:, sl(k * N, nt)],
                                        AF.Identity, bias=0.0,
                                        scale=scb[:, S_RSX2:S_RSX2 + 1])
                return last
            emit("scalar", ["scb1", "x1_spilled"], x8fn, s_act, 1, key=f"x8_k{k}")
        else:
            def x8fn(e, k=k):
                last = None
                for nt in range(NT):
                    o = nt * (KC * NTW) + k * NTW
                    last = e.tensor_scalar_mul(x8v[:, o:o + NTW], bigA_h[:, sl(k * N, nt)],
                                               scb[:, S_RSX2:S_RSX2 + 1])
                return last
            emit("vector", ["scb1", "x1_spilled"], x8fn, s_dve, 1, key=f"x8_k{k}")
    EV["x8_done"] = EV[f"x8_k{KC-1}"]

    def habs_emit(mm):
        pq = mm % 2

        def habs(e, mm=mm, pq=pq):
            return e.reduce_max(ham[:, mm:mm + 1], hbuf[pq][:, :], axis=AX.X,
                                apply_absolute_value=True)
        emit("vector", [f"hev{mm}"], habs, s_dve, 1, key=f"hmx{mm}")
        consumed_add(f"hbuf{pq}", f"hmx{mm}")

    for m in range(FC):
        pp = m % 2
        emit("sync", free_waits(f"wfbuf{pp}") + ([f"omm{MCH-1}"] if m < 2 else []),
             lambda e, m=m, pp=pp: e.dma_start(out=wfbuf[pp], in_=wfcT[m]),
             swf[pp], 16, key=f"wf_ld{m}")
        emit("vector", [f"wf_ld{m}", "scb1"] + free_waits(f"wf8b{pp}"),
             lambda e, pp=pp: e.tensor_scalar_mul(wf8b[pp], wfbuf[pp],
                                                  scb[:, S_RSW2:S_RSW2 + 1]),
             s_dve, 1, key=f"wf8_{m}")
        consumed(f"wfbuf{pp}", f"wf8_{m}")
        if m >= 2:
            habs_emit(m - 2)
        base = 0 if pp == 0 else 4

        def hmm_pair(e, c, pp, base):
            wap = wf8b[pp][:, 256 * c:256 * (c + 1)].rearrange('p (a b) -> p a b', a=2)
            last = None
            for nt in range(NT):
                o = nt * (KC * NTW) + 2 * c * NTW
                xap = x8v[:, o:o + 2 * NTW].rearrange('p (a b) -> p a b', a=2)
                last = e.matmul(pb[base + nt][:, 0:NTW], wap, xap,
                                start=(c == 0), stop=(c == KC // 2 - 1),
                                perf_mode=PM_DR)
            return last

        if m < 2:
            pbfree = []
            for nt in range(NT):
                pbfree += free_waits(f"pb{base + nt}")
            for c in range(KC // 2):
                tw = [f"wf8_{m}", f"x8_k{2*c}", f"x8_k{2*c+1}"] + (pbfree if c == 0 else [])
                emit("tensor", tw,
                     lambda e, c=c, pp=pp, base=base: hmm_pair(e, c, pp, base),
                     s_te, 1, key=f"hmm{m}_c{c}")
            EV[f"hmm{m}"] = EV[f"hmm{m}_c{KC//2-1}"]
        else:
            tw = [f"wf8_{m}", "x8_done"]
            for nt in range(NT):
                tw += free_waits(f"pb{base + nt}")

            def hmm(e, pp=pp, base=base):
                last = None
                for c in range(KC // 2):
                    last = hmm_pair(e, c, pp, base)
                return last
            emit("tensor", tw, hmm, s_te, 1, key=f"hmm{m}")
        consumed(f"wf8b{pp}", f"hmm{m}")

        def hev(e, pp=pp, base=base):
            for nt in range(NT):
                e.activation(hbuf[pp][:, nt * NTW:(nt + 1) * NTW],
                             pb[base + nt][:, 0:NTW], AF.Relu,
                             bias=0.0, scale=scb[:, S_DQ1:S_DQ1 + 1])
            return e.activation(hbuf[pp][:, :], hbuf[pp][:, :], AF.Square)
        emit("scalar", [f"hmm{m}"] + free_waits(f"hbuf{pp}"), hev, s_act, 1, key=f"hev{m}")
        for nt in range(NT):
            consumed(f"pb{base + nt}", f"hev{m}")

        emit("scalar", [f"hev{m}"],
             lambda e, m=m, pp=pp: e.dma_start(out=h_sp[m], in_=hbuf[pp]),
             sph[pp], 16, key=f"hsp{m}")
        consumed(f"hbuf{pp}", f"hsp{m}")
        if m == FC - 1:
            habs_emit(FC - 2)
            habs_emit(FC - 1)
    EV["hmax_done"] = EV[f"hmx{FC-1}"]
    dbg_dump("amax4", ham[:, :], [128, 64], F32, "hmax_done")
    dbg_dump("wfam", wfam[:, :], [128, 128], F32, "prepass")
    EV["mm1_sp0"] = EV[f"hsp{FC-2}"]
    EV["mm1_sp1"] = EV[f"hsp{FC-1}"]
    EV["mm1_mm_done"] = EV[f"hmm{FC-1}"]

    emit("vector", ["hmax_done"],
         lambda e: e.reduce_max(amaxf[3][:, :], ham[:, 0:FC], axis=AX.X),
         s_dve, 1, key="hamred")
    emit("tensor", ["hamred"] + free_waits("pb0"),
         lambda e: e.transpose(pb[0][0:1, 0:128], amaxf[3][:, :], idf[:, :]),
         s_te, 1, key="hmx_tr")
    emit("vector", ["hmx_tr"],
         lambda e: e.reduce_max(sc[0:1, S_HMAXL:S_HMAXL + 1], pb[0][0:1, 0:128], axis=AX.X),
         s_dve, 1, key="hmx_sc")
    consumed("pb0", "hmx_sc")
    emit("gpsimd", ["hmx_sc"], lambda e: e.dma_start(out=cc_in2.ap(), in_=sc[0:1, 0:16]),
         s_gld, 16, key="cc2_out")
    emit("gpsimd", ["cc2_out"],
         lambda e: e.collective_compute("AllReduce", ALU.max,
                                        replica_groups=[list(range(n_cores))],
                                        ins=[cc_in2.ap().opt()], outs=[cc_out2.ap().opt()]),
         s_cc, 1, key="cc2")
    emit("gpsimd", ["cc2"],
         lambda e: e.dma_start(out=sc[0:1, S_HMAXG:S_HMAXG + 1],
                               in_=cc_out2.ap()[0:1, S_HMAXL:S_HMAXL + 1]),
         s_gld, 16, key="cc2_back")

    def sc2(e):
        e.tensor_scalar(sc[0:1, S_SH:S_SH + 1], sc[0:1, S_HMAXG:S_HMAXG + 1],
                        1.0 / FP8_MAX, 1e-6, ALU.mult, ALU.max)
        e.tensor_copy(vsink[0:1, :], sc[0:1, S_SP:S_SP + 1])
        e.tensor_copy(vsink[0:1, :], sc[0:1, S_SP:S_SP + 1])
        e.reciprocal(sc[0:1, 1:2], sc[0:1, S_SH:S_SH + 1])
        e.scalar_tensor_tensor(sc[0:1, S_DQ2:S_DQ2 + 1], sc[0:1, S_SH:S_SH + 1], 4.0,
                               sc[0:1, S_SP:S_SP + 1], ALU.mult, ALU.mult)
        e.tensor_copy(vsink[0:1, :], sc[0:1, S_SP:S_SP + 1])
        e.tensor_copy(vsink[0:1, :], sc[0:1, S_SP:S_SP + 1])
        return e.tensor_scalar_mul(sc[0:1, S_SQS:S_SQS + 1], sc[0:1, 1:2], 0.5)
    emit("vector", ["cc2_back"], sc2, s_dve, 1, key="sc2_math")
    emit("tensor", ["sc2_math"] + free_waits("pb0"),
         lambda e: e.matmul(pb[0][:, 0:NSC], onesrow[:, :], sc[0:1, 0:NSC],
                            start=True, stop=True),
         s_te, 1, key="scb2_mm")
    emit("vector", ["scb2_mm"],
         lambda e: e.tensor_copy(scb[:, :], pb[0][:, 0:NSC]),
         s_dve, 1, key="scb2")
    consumed("pb0", "scb2")

    for m in range(FC):
        pp = m % 2
        emit("gpsimd", [f"hsp{m}"] + free_waits(f"xb{pp}"),
             lambda e, m=m, pp=pp: e.dma_start(out=xbh[pp], in_=h_sp[m]),
             sxb[pp], 16, key=f"hrd{m}")
        if m % 2 == 0:
            def h8fn(e, m=m, pp=pp):
                last = None
                for nt in range(NT):
                    o = nt * (FC * NTW) + m * NTW
                    last = e.activation(h8v[:, o:o + NTW],
                                        xbh[pp][:, nt * NTW:(nt + 1) * NTW],
                                        AF.Identity, bias=0.0,
                                        scale=scb[:, S_SQS:S_SQS + 1])
                return last
            emit("scalar", [f"hrd{m}", "scb2"], h8fn, s_act, 1, key=f"h8_{m}")
        else:
            def h8fn(e, m=m, pp=pp):
                last = None
                for nt in range(NT):
                    o = nt * (FC * NTW) + m * NTW
                    last = e.tensor_scalar_mul(h8v[:, o:o + NTW],
                                               xbh[pp][:, nt * NTW:(nt + 1) * NTW],
                                               scb[:, S_SQS:S_SQS + 1])
                return last
            emit("vector", [f"hrd{m}", "scb2"], h8fn, s_dve, 1, key=f"h8_{m}")
        consumed(f"xb{pp}", f"h8_{m}")
    EV["h8_done"] = EV[f"h8_{FC-1}"]

    subw2 = min(2048, FC * 128)
    nsub = (FC * 128) // subw2
    for m in range(MCH):
        mp = m % 2
        for j in range(nsub):
            i = m * nsub + j
            ip = i % 2
            emit("sync", free_waits(f"wpbuf{ip}") + ([f"wf8_{FC-1}"] if i < 2 else []),
                 lambda e, m=m, j=j, ip=ip: e.dma_start(out=wpbuf[ip][:, 0:subw2],
                                                        in_=wprT[m][:, j * subw2:(j + 1) * subw2]),
                 swp[ip], 16, key=f"wp_ld{i}")
            gw = ["mm1_mm_done", "mm1_sp0", "mm1_sp1"] if i < 2 * nsub else []
            emit("vector", [f"wp_ld{i}", "scb1"] + gw
                 + (free_waits(f"wp8b{mp}") if j == 0 else []),
                 lambda e, j=j, ip=ip, mp=mp: e.tensor_scalar_mul(
                     wp8b[mp][:, j * subw2:(j + 1) * subw2], wpbuf[ip][:, 0:subw2],
                     scb[:, S_RSP2:S_RSP2 + 1]),
                 s_dve, 1, key=f"wp8_{i}")
            consumed(f"wpbuf{ip}", f"wp8_{i}")
        base = 0 if mp == 0 else 4

        def o2mm_pair(e, c, mp, base):
            wap = wp8b[mp][:, 256 * c:256 * (c + 1)].rearrange('p (a b) -> p a b', a=2)
            last = None
            for nt in range(NT):
                o = nt * (FC * NTW) + 2 * c * NTW
                xap = h8v[:, o:o + 2 * NTW].rearrange('p (a b) -> p a b', a=2)
                last = e.matmul(pb[base + nt][:, 0:NTW], wap, xap,
                                start=(c == 0), stop=(c == FC // 2 - 1),
                                perf_mode=PM_DR)
            return last

        if m < 2:
            pbfree = []
            for nt in range(NT):
                pbfree += free_waits(f"pb{base + nt}")
            for c in range(FC // 2):
                tw = [f"wp8_{m * nsub + nsub - 1}", f"h8_{2*c}", f"h8_{2*c+1}"] \
                    + (pbfree if c == 0 else [])
                emit("tensor", tw,
                     lambda e, c=c, mp=mp, base=base: o2mm_pair(e, c, mp, base),
                     s_te, 1, key=f"omm2_{m}_c{c}")
            EV[f"omm2_{m}"] = EV[f"omm2_{m}_c{FC//2-1}"]
        else:
            tw = [f"wp8_{m * nsub + nsub - 1}", "h8_done"]
            for nt in range(NT):
                tw += free_waits(f"pb{base + nt}")

            def o2mm(e, mp=mp, base=base):
                last = None
                for c in range(FC // 2):
                    last = o2mm_pair(e, c, mp, base)
                return last
            emit("tensor", tw, o2mm, s_te, 1, key=f"omm2_{m}")
        consumed(f"wp8b{mp}", f"omm2_{m}")

        def o2ev(e, mp=mp, base=base):
            last = None
            for nt in range(NT):
                last = e.activation(obuf[mp][:, nt * NTW:(nt + 1) * NTW],
                                    pb[base + nt][:, 0:NTW], AF.Identity,
                                    bias=0.0, scale=scb[:, S_DQ2:S_DQ2 + 1])
            return last
        emit("scalar", [f"omm2_{m}"] + free_waits(f"obuf{mp}"), o2ev, s_act, 1, key=f"oev2_{m}")
        for nt in range(NT):
            consumed(f"pb{base + nt}", f"oev2_{m}")
        emit("scalar", [f"oev2_{m}"],
             lambda e, m=m, mp=mp: e.dma_start(out=o_sp[m], in_=obuf[mp]),
             spo[mp], 16, key=f"osp{m}")
        consumed(f"obuf{mp}", f"osp{m}")
    EV["mm2_sp0"] = EV[f"osp{MCH-2}"]
    EV["mm2_sp1"] = EV[f"osp{MCH-1}"]

    ln_stats("ln3", "osp")
    ln_bcasts("ln3")
    for k in range(KC):
        pp = k % 2
        emit("sync", [f"osp{k}"] + free_waits(f"xb{pp}"),
             lambda e, k=k, pp=pp: e.dma_start(out=xbh[pp], in_=o_sp[k]),
             sxb[pp], 16, key=f"ord{k}")
        emit("gpsimd", free_waits(f"x1b{pp}")
             + (["x1_spilled", f"omm2_{MCH-1}"] if k < 2 else []),
             lambda e, k=k, pp=pp: e.dma_start(out=x1b[pp], in_=x1_sp[k]),
             sx1b[pp], 16, key=f"x1rd{k}")

        def ofn(e, k=k, pp=pp):
            last = None
            for nt in range(NT):
                e.tensor_tensor(t32a[:, 0:NTW], xbh[pp][:, nt * NTW:(nt + 1) * NTW],
                                pb[nt][:, 0:NTW], ALU.subtract)
                e.tensor_tensor(t32b[:, 0:NTW], t32a[:, 0:NTW], pb[4 + nt][:, 0:NTW], ALU.mult)
                last = e.tensor_tensor(outb[pp][:, nt * NTW:(nt + 1) * NTW], t32b[:, 0:NTW],
                                       x1b[pp][:, nt * NTW:(nt + 1) * NTW], ALU.add)
            return last
        emit("vector", [f"ord{k}", f"x1rd{k}", "ln3_bc"] + free_waits(f"outb{pp}"),
             ofn, s_dve, 1, key=f"out_k{k}")
        consumed(f"xb{pp}", f"out_k{k}")
        consumed(f"x1b{pp}", f"out_k{k}")
        emit("sync" if k % 2 == 0 else "gpsimd", [f"out_k{k}"],
             lambda e, k=k, pp=pp: e.dma_start(out=outT[k], in_=outb[pp]),
             sout[pp], 16, key=f"st{k}")
        consumed(f"outb{pp}", f"st{k}")

    emit("sync", [f"st{KC-2}", f"st{KC-1}"], lambda e: e.nop())
    emit("tensor", [f"st{KC-2}", f"st{KC-1}"], lambda e: e.nop())

    def replay(eng_name):
        def fn(e):
            for wl, build, semx, n in recs[eng_name]:
                for s, v in wl:
                    e.wait_ge(s, v)
                instr = build(e)
                if semx is not None:
                    instr.then_inc(semx, n)
        return fn

    with nc.Block() as B:
        B.gpsimd(replay("gpsimd"))
        B.sync(replay("sync"))
        B.tensor(replay("tensor"))
        B.scalar(replay("scalar"))
        B.vector(replay("vector"))

    es.close()
    return nc


_D, _N, _H, _F, _NCORES = 2048, 2048, 16, 8192, 8
_nc_cache = {}


def _tile_wT(w):
    d_out, d_in = w.shape
    MCH, KC = d_out // 128, d_in // 128
    t = np.asarray(w, dtype=np.float32).reshape(MCH, 128, KC, 128)
    t = t.transpose(0, 3, 2, 1)
    return np.ascontiguousarray(t.reshape(MCH, 128, KC * 128))


def kernel(L0_attn_x, L0_attn_Wq, L0_attn_bq, L0_attn_Wk, L0_attn_bk,
           L0_attn_Wv, L0_attn_bv, L0_attn_Wo, L0_attn_bo,
           L0_ffn_w_fc, L0_ffn_w_proj):
    D, N, H, F, C = _D, _N, _H, _F, _NCORES
    KC = D // 128
    TB = N // 128

    x = np.ascontiguousarray(np.asarray(L0_attn_x, dtype=np.float32))
    assert x.shape == (C, N, D)

    if "nc" not in _nc_cache:
        _nc_cache["nc"] = build_nc(D, N, H, F, C)
    nc = _nc_cache["nc"]

    shared = dict(
        wqT=_tile_wT(L0_attn_Wq), wkT=_tile_wT(L0_attn_Wk), wvT=_tile_wT(L0_attn_Wv),
        woT=_tile_wT(L0_attn_Wo), wfcT=_tile_wT(L0_ffn_w_fc), wprT=_tile_wT(L0_ffn_w_proj),
    )

    def col(b):
        return np.ascontiguousarray(np.asarray(b, dtype=np.float32).reshape(KC, 128).T)

    pos = np.arange(N, dtype=np.float32)
    decay = np.exp(-pos / np.float32(N)).astype(np.float32)
    shared.update(
        bq_c=col(L0_attn_bq), bk_c=col(L0_attn_bk), bv_c=col(L0_attn_bv),
        bo_c=col(L0_attn_bo),
        identf=np.eye(128, dtype=np.float32),
        identb=np.eye(128, dtype=ml_dtypes.bfloat16),
        decrow_d=decay.reshape(1, N),
        deccol_d=np.ascontiguousarray(decay.reshape(TB, 128).T),
    )

    FC8 = (F // 128) // C
    M8 = (D // 128) // C
    in_maps = []
    for c in range(C):
        m = dict(shared)
        m["xT"] = np.ascontiguousarray(x[c].T).reshape(KC, 128, N)
        m["wfsc"] = shared["wfcT"][c * FC8:(c + 1) * FC8]
        m["wpsc"] = shared["wprT"][c * M8:(c + 1) * M8]
        in_maps.append(m)

    res = run_bass_kernel_spmd(nc, in_maps, core_ids=list(range(C)))
    out = np.empty((C, N, D), dtype=np.float32)
    for c in range(C):
        out[c] = res.results[c]["outT"].reshape(D, N).T
    return out


# version 12
# speedup vs baseline: 1.0451x; 1.0451x over previous
import numpy as np
import ml_dtypes

import concourse.bass as bass
import concourse.mybir as mybir
from concourse.bass_utils import run_bass_kernel_spmd

F32 = mybir.dt.float32
BF16 = mybir.dt.bfloat16
F16 = mybir.dt.float16
F8 = mybir.dt.float8e4
U8 = mybir.dt.uint8
AF = mybir.ActivationFunctionType
ALU = mybir.AluOpType
AX = mybir.AxisListType
PM_DR = mybir.MatmulPerfMode.DoubleRow

FP8_MAX = 448.0
EPS_ATTN = 1e-6
EPS_LN = 1e-5

S_XMAXL, S_WFMAX, S_WPMAX, S_HMAXL = 0, 1, 2, 3
S_XMAXG, S_HMAXG = 4, 5
S_SX, S_SW, S_SP, S_SH = 6, 7, 8, 9
S_RSX2, S_RSW2, S_RSP2, S_DQ1, S_SQS, S_DQ2 = 10, 11, 12, 13, 14, 15
NSC = 16

ENGS = ("gpsimd", "sync", "tensor", "scalar", "vector")


def build_nc(D, N, H, F, n_cores, debug=False):
    KC = D // 128
    MCH = D // 128
    FC = F // 128
    TB = N // 128
    NTW = min(512, N)
    NT = N // NTW
    assert H == KC and FC == 4 * KC

    nc = bass.Bass("TRN2", target_bir_lowering=False,
                   detect_race_conditions=False, dynamic_dma_scratch_size=16384)

    xT = nc.dram_tensor("xT", [KC, 128, N], F32, kind="ExternalInput").ap()
    wfsc = nc.dram_tensor("wfsc", [FC // n_cores, 128, KC * 128], F32,
                          kind="ExternalInput").ap()
    wpsc = nc.dram_tensor("wpsc", [MCH // n_cores, 128, FC * 128], F32,
                          kind="ExternalInput").ap()
    wqT = nc.dram_tensor("wqT", [MCH, 128, KC * 128], F32, kind="ExternalInput").ap()
    wkT = nc.dram_tensor("wkT", [MCH, 128, KC * 128], F32, kind="ExternalInput").ap()
    wvT = nc.dram_tensor("wvT", [MCH, 128, KC * 128], F32, kind="ExternalInput").ap()
    woT = nc.dram_tensor("woT", [MCH, 128, KC * 128], F32, kind="ExternalInput").ap()
    wfcT = nc.dram_tensor("wfcT", [FC, 128, KC * 128], F32, kind="ExternalInput").ap()
    wprT = nc.dram_tensor("wprT", [MCH, 128, FC * 128], F32, kind="ExternalInput").ap()
    bq_d = nc.dram_tensor("bq_c", [128, MCH], F32, kind="ExternalInput").ap()
    bk_d = nc.dram_tensor("bk_c", [128, MCH], F32, kind="ExternalInput").ap()
    bv_d = nc.dram_tensor("bv_c", [128, MCH], F32, kind="ExternalInput").ap()
    bo_d = nc.dram_tensor("bo_c", [128, MCH], F32, kind="ExternalInput").ap()
    identf = nc.dram_tensor("identf", [128, 128], F32, kind="ExternalInput").ap()
    identb = nc.dram_tensor("identb", [128, 128], BF16, kind="ExternalInput").ap()
    decrow_d = nc.dram_tensor("decrow_d", [1, N], F32, kind="ExternalInput").ap()
    deccol_d = nc.dram_tensor("deccol_d", [128, TB], F32, kind="ExternalInput").ap()
    outT = nc.dram_tensor("outT", [KC, 128, N], F32, kind="ExternalOutput").ap()

    dbg = dict(kind="ExternalOutput") if debug else {}
    h_sp = nc.dram_tensor("h_sp", [FC, 128, N], F16, **dbg).ap()
    x1_sp = nc.dram_tensor("x1_sp", [KC, 128, N], BF16, **dbg).ap()
    o_sp = nc.dram_tensor("o_sp", [KC, 128, N], F16, **dbg).ap()
    sc_dram = nc.dram_tensor("sc_dram", [1, NSC], F32, **dbg).ap()
    cc_in1 = nc.dram_tensor("cc_in1", [1, 16], F32)
    cc_out1 = nc.dram_tensor("cc_out1", [1, 16], F32)
    cc_in2 = nc.dram_tensor("cc_in2", [1, 16], F32)
    cc_out2 = nc.dram_tensor("cc_out2", [1, 16], F32)

    from contextlib import ExitStack
    es = ExitStack()
    sb = lambda name, shape, dt: es.enter_context(nc.sbuf_tensor(name, shape, dt))
    psm = lambda name, shape: es.enter_context(nc.psum_tensor(name, shape, F32))
    mksem = lambda name: es.enter_context(nc.semaphore(name))

    mega = sb("mega", [128, 4 * KC * N], U8)
    bigA = mega[:, 0:KC * N * 2].bitcast(BF16)
    bigA_h = mega[:, 0:KC * N * 2].bitcast(F16)
    bigB = mega[:, KC * N * 2:4 * KC * N].bitcast(BF16)
    x8v = mega[:, KC * N * 2:KC * N * 3].bitcast(F8)
    h8v = mega[:, :].bitcast(F8)

    M2B = 48 * 1024
    mega2 = sb("mega2", [128, M2B], U8)

    def m2(lo, nbytes, dt):
        assert lo + nbytes <= M2B, (lo, nbytes)
        return mega2[:, lo:lo + nbytes].bitcast(dt)

    wbuf = [m2(0, KC * 256, BF16), m2(KC * 256, KC * 256, BF16)]
    _o2 = KC * 512
    kfm = m2(_o2, 2 * N, BF16)
    vfm = m2(_o2 + 2 * N, 2 * N, BF16)
    _o3 = _o2 + 4 * N
    ktok = [m2(_o3 + i * 512, 256, BF16) for i in range(4)]
    vtok = [m2(_o3 + i * 512 + 256, 256, BF16) for i in range(4)]
    kvsb = m2(_o3 + 2048, 256, BF16)
    ksumc = m2(_o3 + 2304, 2, BF16)
    _o4 = _o3 + 2560
    zbsb = [m2(_o4, NTW * 4, F32), m2(_o4 + NTW * 4, NTW * 4, F32)]
    _o5 = _o4 + NTW * 8
    pre = [m2(_o5, 4096, F32), m2(_o5 + 4096, 4096, F32)]
    _o6 = _o5 + 8192
    rowC = m2(_o6, 4 * N, F32)[0:1, :]
    rowD = m2(_o6 + 4 * N, 4 * N, F32)[0:1, :]
    assert _o6 + 8 * N <= M2B
    wfbuf = [m2(0, KC * 512, F32), m2(KC * 512, KC * 512, F32)]
    _f1 = KC * 1024
    wf8b = [m2(_f1, KC * 128, F8), m2(_f1 + KC * 128, KC * 128, F8)]
    _f2 = _f1 + 2 * KC * 128
    hbuf = [m2(_f2, 2 * N, F16), m2(_f2 + 2 * N, 2 * N, F16)]
    wpbuf = [m2(0, 8192, F32), m2(8192, 8192, F32)]
    wp8b = [m2(16384, FC * 128, F8), m2(16384 + FC * 128, FC * 128, F8)]
    _m2o = 16384 + 2 * FC * 128
    obuf = [m2(_m2o, 2 * N, F16), m2(_m2o + 2 * N, 2 * N, F16)]
    outb = [m2(0, 4 * N, F32), m2(4 * N, 4 * N, F32)]
    x1b = [m2(8 * N, 2 * N, BF16), m2(10 * N, 2 * N, BF16)]
    _sq = M2B - 4 * N
    sqb = [m2(_sq, 2 * N, BF16), m2(_sq + 2 * N, 2 * N, BF16)]
    sqh = [m2(_sq, 2 * N, F16), m2(_sq + 2 * N, 2 * N, F16)]
    assert _sq >= 12 * N and _sq >= _m2o + 4 * N and _sq >= _f2 + 4 * N
    assert _sq >= _o6 + 4 * N

    xb = [sb("xb0", [128, N], BF16), sb("xb1", [128, N], BF16)]
    xbh = [xb[0][:, :].bitcast(F16), xb[1][:, :].bitcast(F16)]
    t32a = sb("t32a", [128, NTW], F32)
    t32b = sb("t32b", [128, NTW], F32)
    rowA = sb("rowA", [1, N], F32)
    rowB = sb("rowB", [1, N], F32)
    idf = sb("idf", [128, 128], F32)
    idb = sb("idb", [128, 128], BF16)
    onesrow = sb("onesrow", [1, 128], F32)
    ones1f = sb("ones1f", [1, 1], F32)
    onescol = sb("onescol", [128, 1], BF16)
    onescol_h = sb("onescol_h", [128, 1], F16)
    sc = sb("sc", [1, NSC], F32)
    scb = sb("scb", [128, NSC], F32)
    wfam = sb("wfam", [128, 128], F32)
    wpam = sb("wpam", [128, 128], F32)
    xam = sb("xam", [128, 64], F32)
    ham = sb("ham", [128, 64], F32)
    amaxf = [sb(f"amaxf{i}", [128, 1], F32) for i in range(4)]
    prem = [sb("prem0", [128, 1], F32), sb("prem1", [128, 1], F32)]
    vsink = sb("vsink", [128, 1], F32)
    asink = sb("asink", [128, 1], F32)
    biasq = sb("biasq", [128, MCH], F32)
    biask = sb("biask", [128, MCH], F32)
    biasv = sb("biasv", [128, MCH], F32)
    biaso = sb("biaso", [128, MCH], F32)
    deccol = sb("deccol", [128, TB], F32)

    pb = [psm(f"pb{j}", [128, 512]) for j in range(8)]
    pbb = {j: pb[j][:, :].bitcast(BF16) for j in (0, 1, 6, 7)}

    s_gld = mksem("s_gld")
    s_te = mksem("s_te")
    s_dve = mksem("s_dve")
    s_act = mksem("s_act")
    s_cc = mksem("s_cc")
    sconst = mksem("sconst")
    sxb = [mksem("sxb0"), mksem("sxb1")]
    sxg = [mksem("sxg0"), mksem("sxg1")]
    swb = [mksem("swb0"), mksem("swb1")]
    swf = [mksem("swf0"), mksem("swf1")]
    swp = [mksem("swp0"), mksem("swp1")]
    spre = [mksem("spre0"), mksem("spre1")]
    sx1b = [mksem("sx1b0"), mksem("sx1b1")]
    sph = [mksem("sph0"), mksem("sph1")]
    spo = [mksem("spo0"), mksem("spo1")]
    spx = mksem("spx")
    sout = [mksem("sout0"), mksem("sout1")]
    ALLSEMS = [s_gld, s_te, s_dve, s_act, s_cc, sconst, spx] + \
        sxb + sxg + swb + swf + swp + spre + sx1b + sph + spo + sout

    recs = {e: [] for e in ENGS}
    CNT = {s.name: 0 for s in ALLSEMS}
    EV = {}
    READY = {}
    FREE = {}

    def emit(eng, waits, build, sem=None, n=1, key=None):
        wl = []
        seen = set()
        for w in waits:
            if w is None:
                continue
            s, v = EV[w] if isinstance(w, str) else w
            if (s.name, v) in seen:
                continue
            seen.add((s.name, v))
            wl.append((s, v))
        if sem is not None:
            CNT[sem.name] += n
            val = CNT[sem.name]
            if key is not None:
                EV[key] = (sem, val)
            recs[eng].append((wl, build, sem, n))
            return key
        recs[eng].append((wl, build, None, 0))
        return None

    def produce(res, key):
        READY[res] = key

    def ready(res):
        return READY.get(res)

    def consumed(res, key):
        FREE[res] = [key]

    def consumed_add(res, key):
        FREE.setdefault(res, []).append(key)

    def free_waits(res):
        w = FREE.get(res, [])
        FREE[res] = []
        return w

    def sl(base, nt):
        return slice(base + nt * NTW, base + (nt + 1) * NTW)

    DBG_BUFS = {}

    def dbg_dump(tag, src_ap, shape, dt, wait_key):
        if debug != 2:
            return
        t = nc.dram_tensor(f"d_{tag}", shape, dt, kind="ExternalOutput").ap()
        DBG_BUFS[tag] = t
        key = f"dbg_{tag}"
        emit("gpsimd", [wait_key],
             lambda e: e.dma_start(out=t, in_=src_ap, allow_slow_non_contiguous=True),
             s_gld, 16, key=key)
        return key

    const_pairs = [(idf[:, :], identf), (idb[:, :], identb), (biasq[:, :], bq_d),
                   (biask[:, :], bk_d), (biasv[:, :], bv_d), (biaso[:, :], bo_d),
                   (deccol[:, 0:TB], deccol_d)]
    for i, (dst, src) in enumerate(const_pairs):
        emit("gpsimd", [], lambda e, dst=dst, src=src: e.dma_start(out=dst, in_=src),
             sconst, 16, key=f"c{i}")
    EV["consts"] = EV[f"c{len(const_pairs)-1}"]
    emit("vector", [], lambda e: e.memset(onesrow[:, :], 1.0))
    emit("vector", [], lambda e: e.memset(ones1f[:, :], 1.0))
    emit("vector", [], lambda e: e.memset(onescol[:, :], 1.0))
    emit("vector", [], lambda e: e.memset(onescol_h[:, :], 1.0))
    emit("vector", [], lambda e: e.memset(sc[:, :], 0.0))
    emit("vector", [], lambda e: e.memset(wfam[:, :], 0.0))
    emit("vector", [], lambda e: e.memset(wpam[:, :], 0.0))
    emit("vector", [], lambda e: e.memset(xam[:, :], 0.0))
    emit("vector", [], lambda e: e.memset(ham[:, :], 0.0))
    emit("vector", [], lambda e: e.memset(scb[:, :], 0.0), s_dve, 1, key="ones_done")
    for _e in ("sync", "tensor", "scalar", "vector"):
        emit(_e, ["consts", "ones_done"], lambda e: e.nop())

    def ln_stats(tag, kind, src_ready_key=None, src_key_fn=None):
        f16src = kind == "osp"
        oc = onescol_h if f16src else onescol
        for k in range(KC):
            pp = k % 2
            if kind == "x0":
                emit("gpsimd", free_waits(f"xb{pp}"),
                     lambda e, k=k, pp=pp: e.dma_start(out=xb[pp][:, :], in_=xT[k]),
                     sxg[pp], 16, key=f"{tag}_ld{k}")
                srcap = xb[pp][:, :]
                rk = f"{tag}_ld{k}"
            elif kind == "osp":
                emit("sync", [f"osp{k}"] + free_waits(f"xb{pp}"),
                     lambda e, k=k, pp=pp: e.dma_start(out=xbh[pp], in_=o_sp[k]),
                     sxb[pp], 16, key=f"{tag}_ld{k}")
                srcap = xbh[pp]
                rk = f"{tag}_ld{k}"
            else:
                srcap = (bigA if kind == "bigA" else bigB)[:, k * N:(k + 1) * N]
                rk = src_key_fn(k) if src_key_fn else src_ready_key
            sq = sqh[pp] if f16src else sqb[pp]
            hw = N // 2
            sqfree = free_waits(f"sq{pp}")
            emit("scalar", [rk] + sqfree,
                 lambda e, srcap=srcap, sq=sq, hw=hw:
                     e.activation(sq[:, 0:hw], srcap[:, 0:hw], AF.Square),
                 s_act, 1, key=f"{tag}_sqa{k}")
            if f16src:
                def sqv(e, srcap=srcap, sq=sq, hw=hw):
                    return e.tensor_mul(sq[:, hw:N], srcap[:, hw:N], srcap[:, hw:N])
            else:
                def sqv(e, srcap=srcap, sq=sq, hw=hw):
                    return e.tensor_mul(sq[:, hw:N], srcap[:, hw:N], srcap[:, hw:N])
            emit("vector", [rk] + sqfree, sqv, s_dve, 1, key=f"{tag}_sqb{k}")
            produce(f"sq{pp}", f"{tag}_sqa{k}")
            tw = [rk, f"{tag}_sqa{k}", f"{tag}_sqb{k}", "ones_done"]
            if k == 0:
                for j in range(8):
                    tw += free_waits(f"pb{j}")

            def mmfn(e, k=k, srcap=srcap, sq=sq, oc=oc):
                last = None
                for nt in range(NT):
                    e.matmul(pb[nt][0:1, 0:NTW], oc[:, :],
                             srcap[:, nt * NTW:(nt + 1) * NTW],
                             start=(k == 0), stop=(k == KC - 1))
                    last = e.matmul(pb[4 + nt][0:1, 0:NTW], oc[:, :],
                                    sq[:, nt * NTW:(nt + 1) * NTW],
                                    start=(k == 0), stop=(k == KC - 1))
                return last
            emit("tensor", tw, mmfn, s_te, 1, key=f"{tag}_mm{k}")
            consumed(f"sq{pp}", f"{tag}_mm{k}")
            if kind in ("x0", "osp"):
                consumed(f"xb{pp}", f"{tag}_mm{k}")

        def rowsfn(e):
            for nt in range(NT):
                e.tensor_copy(rowA[0:1, nt * NTW:(nt + 1) * NTW], pb[nt][0:1, 0:NTW])
                e.tensor_copy(rowB[0:1, nt * NTW:(nt + 1) * NTW], pb[4 + nt][0:1, 0:NTW])
            e.tensor_scalar_mul(rowA[:, :], rowA[:, :], 1.0 / D)
            e.tensor_scalar_mul(rowB[:, :], rowB[:, :], 1.0 / D)
            e.tensor_mul(rowC, rowA[:, :], rowA[:, :])
            e.tensor_sub(rowB[:, :], rowB[:, :], rowC)
            e.tensor_scalar_add(rowB[:, :], rowB[:, :], EPS_LN)
            return e.tensor_copy(vsink[0:1, :], rowB[0:1, 0:1])
        emit("vector", [f"{tag}_mm{KC-1}"], rowsfn, s_dve, 1, key=f"{tag}_var")
        for j in range(8):
            consumed(f"pb{j}", f"{tag}_var")
        def sqrtfn(e):
            e.activation(rowB[:, :], rowB[:, :], AF.Sqrt)
            return e.copy(asink[0:1, :], rowB[0:1, 0:1])
        emit("scalar", [f"{tag}_var"], sqrtfn, s_act, 1, key=f"{tag}_sqrt")
        def recfn(e):
            e.reciprocal(rowB[:, :], rowB[:, :])
            return e.tensor_copy(vsink[0:1, :], rowB[0:1, 0:1])
        emit("vector", [f"{tag}_sqrt"], recfn, s_dve, 1, key=f"{tag}_rows")

    def ln_bcasts(tag):
        w = [f"{tag}_rows"]
        for j in range(8):
            w += free_waits(f"pb{j}")

        def bfn(e):
            last = None
            for nt in range(NT):
                e.matmul(pb[nt][:, 0:NTW], onesrow[:, :],
                         rowA[0:1, nt * NTW:(nt + 1) * NTW], start=True, stop=True)
                last = e.matmul(pb[4 + nt][:, 0:NTW], onesrow[:, :],
                                rowB[0:1, nt * NTW:(nt + 1) * NTW], start=True, stop=True)
            return last
        emit("tensor", w, bfn, s_te, 1, key=f"{tag}_bc")

    ln_stats("ln0", "x0")
    emit("gpsimd", [f"ln0_mm{KC-1}"], lambda e: e.dma_start(out=rowD, in_=decrow_d),
         sconst, 16, key="ld_decrow")

    def epsdfn(e):
        e.reciprocal(rowD, rowD)
        return e.tensor_scalar_mul(rowD, rowD, EPS_ATTN)
    emit("vector", ["ld_decrow"], epsdfn, s_dve, 1, key="epsd")
    ln_bcasts("ln0")
    for k in range(KC):
        pp = k % 2
        emit("gpsimd", free_waits(f"xb{pp}"),
             lambda e, k=k, pp=pp: e.dma_start(out=xb[pp][:, :], in_=xT[k]),
             sxg[pp], 16, key=f"ln0a_ld{k}")

        def apfn(e, k=k, pp=pp):
            last = None
            for nt in range(NT):
                e.tensor_tensor(t32a[:, 0:NTW], xb[pp][:, nt * NTW:(nt + 1) * NTW],
                                pb[nt][:, 0:NTW], ALU.subtract)
                last = e.tensor_tensor(bigA[:, sl(k * N, nt)], t32a[:, 0:NTW],
                                       pb[4 + nt][:, 0:NTW], ALU.mult)
            return last
        emit("vector", [f"ln0a_ld{k}", "ln0_bc"], apfn, s_dve, 1, key=f"ln0a_k{k}")
        consumed(f"xb{pp}", f"ln0a_k{k}")
    EV["ln0_done"] = EV[f"ln0a_k{KC-1}"]
    for j in range(8):
        consumed(f"pb{j}", f"ln0a_k{KC-1}")

    def prepass():
        sub = 1024
        items = [("prf", wfsc, FC // n_cores, (KC * 128) // sub, wfam),
                 ("prp", wpsc, MCH // n_cores, (FC * 128) // sub, wpam)]
        idx = 0
        for tg, src, M, nsub, acc in items:
            assert M * nsub <= 64
            for i in range(M * nsub):
                m, j = divmod(i, nsub)
                pp = idx % 2
                idx += 1
                emit("sync", free_waits(f"pre{pp}"),
                     lambda e, m=m, j=j, pp=pp, src=src:
                         e.dma_start(out=pre[pp][:, 0:sub], in_=src[m][:, j * sub:(j + 1) * sub]),
                     spre[pp], 16, key=f"{tg}_ld{i}")

                def rfn(e, i=i, pp=pp, acc=acc):
                    return e.reduce_max(acc[:, i:i + 1], pre[pp][:, 0:sub], axis=AX.X,
                                        apply_absolute_value=True)
                emit("vector", [f"{tg}_ld{i}"], rfn, s_dve, 1, key=f"{tg}_{i}")
                consumed(f"pre{pp}", f"{tg}_{i}")
        EV["prepass"] = EV[f"prp_{(MCH // n_cores) * ((FC * 128) // sub) - 1}"]

    prepass()


    for m in range(MCH):
        pp = m % 2
        emit("gpsimd", free_waits(f"wbuf{pp}"),
             lambda e, m=m, pp=pp: e.dma_start(out=wbuf[pp], in_=wqT[m]),
             swb[pp], 16, key=f"qw{m}")
        base = 0 if pp == 0 else 4

        def qmm_k(e, k, pp, base):
            last = None
            for nt in range(NT):
                last = e.matmul(pb[base + nt][:, 0:NTW], wbuf[pp][:, 128 * k:128 * (k + 1)],
                                bigA[:, sl(k * N, nt)], start=(k == 0), stop=(k == KC - 1))
            return last

        if m == 0:
            pbfree = []
            for nt in range(NT):
                pbfree += free_waits(f"pb{base + nt}")
            for k in range(KC):
                tw = [f"qw{m}", f"ln0a_k{k}"] + (pbfree if k == 0 else [])
                emit("tensor", tw, lambda e, k=k, pp=pp, base=base: qmm_k(e, k, pp, base),
                     s_te, 1, key=f"qmm{m}_k{k}")
            EV[f"qmm{m}"] = EV[f"qmm{m}_k{KC-1}"]
        else:
            tw = [f"qw{m}", "ln0_done"]
            for nt in range(NT):
                tw += free_waits(f"pb{base + nt}")

            def qmm(e, m=m, pp=pp, base=base):
                last = None
                for k in range(KC):
                    last = qmm_k(e, k, pp, base)
                return last
            emit("tensor", tw, qmm, s_te, 1, key=f"qmm{m}")
        consumed(f"wbuf{pp}", f"qmm{m}")

        def qev(e, m=m, base=base):
            last = None
            for nt in range(NT):
                last = e.activation(bigB[:, sl(m * N, nt)], pb[base + nt][:, 0:NTW],
                                    AF.Relu, bias=biasq[:, m:m + 1], scale=1.0)
            return last
        emit("scalar", [f"qmm{m}"], qev, s_act, 1, key=f"qev{m}")
        for nt in range(NT):
            consumed(f"pb{base + nt}", f"qev{m}")
    EV["q_done"] = EV[f"qev{MCH-1}"]
    dbg_dump("q", bigB, [128, KC * N], BF16, "q_done")

    for h in range(H):
        emit("gpsimd", free_waits("wbuf0"),
             lambda e, h=h: e.dma_start(out=wbuf[0], in_=wkT[h]), swb[0], 16, key=f"wk{h}")
        emit("gpsimd", free_waits("wbuf1"),
             lambda e, h=h: e.dma_start(out=wbuf[1], in_=wvT[h]), swb[1], 16, key=f"wv{h}")
        for which, wkey, outt, biast, func, base in (
                ("k", f"wk{h}", kfm, biask, AF.Relu, 0),
                ("v", f"wv{h}", vfm, biasv, AF.Identity, 4)):
            wb = wbuf[0] if which == "k" else wbuf[1]
            tw = [wkey] + free_waits(which + "fm")
            for nt in range(NT):
                tw += free_waits(f"pb{base + nt}")

            def pmm(e, wb=wb, base=base):
                last = None
                for k in range(KC):
                    for nt in range(NT):
                        last = e.matmul(pb[base + nt][:, 0:NTW], wb[:, 128 * k:128 * (k + 1)],
                                        bigA[:, sl(k * N, nt)], start=(k == 0), stop=(k == KC - 1))
                return last
            emit("tensor", tw, pmm, s_te, 1, key=f"{which}mm_{h}")
            for nt in range(NT):
                emit("scalar", [f"{which}mm_{h}"],
                     lambda e, nt=nt, base=base, outt=outt, biast=biast, func=func, h=h:
                         e.activation(outt[:, nt * NTW:(nt + 1) * NTW], pb[base + nt][:, 0:NTW],
                                      func, bias=biast[:, h:h + 1], scale=1.0),
                     s_act, 1, key=f"{which}ev_{h}_{nt}")
                consumed(f"pb{base + nt}", f"{which}ev_{h}_{nt}")
            consumed(f"wbuf{0 if which == 'k' else 1}", f"{which}mm_{h}")
        for tb in range(TB):
            bnk = (0, 1, 6, 7)[tb % 4]
            sl4 = tb % 4
            tw = [f"kev_{h}_{NT-1}", f"vev_{h}_{NT-1}"] + free_waits(f"pb{bnk}")
            if tb == 0:
                tw += free_waits("pb2")

            def trfn(e, tb=tb, bnk=bnk):
                e.transpose(pbb[bnk][:, 0:128], kfm[:, 128 * tb:128 * (tb + 1)], idb[:, :])
                return e.transpose(pbb[bnk][:, 128:256], vfm[:, 128 * tb:128 * (tb + 1)], idb[:, :])
            emit("tensor", tw, trfn, s_te, 1, key=f"tr_{h}_{tb}")

            def tokfn(e, tb=tb, bnk=bnk, sl4=sl4):
                e.tensor_scalar_mul(ktok[sl4][:, :], pbb[bnk][:, 0:128], deccol[:, tb:tb + 1])
                return e.tensor_copy(vtok[sl4][:, :], pbb[bnk][:, 128:256])
            emit("vector", [f"tr_{h}_{tb}", "consts"] + free_waits(f"ktok{sl4}"),
                 tokfn, s_dve, 1, key=f"tok_{h}_{tb}")
            consumed(f"pb{bnk}", f"tok_{h}_{tb}")

            def kvfn(e, tb=tb, sl4=sl4):
                e.matmul(pb[2][:, 0:128], ktok[sl4][:, :], vtok[sl4][:, :],
                         start=(tb == 0), stop=False)
                return e.matmul(pb[2][:, 128:129], ktok[sl4][:, :], onescol[:, :],
                                start=False, stop=(tb == TB - 1))
            emit("tensor", [f"tok_{h}_{tb}"], kvfn, s_te, 1,
                 key=f"kvmm_{h}_{tb}")
            consumed(f"ktok{sl4}", f"kvmm_{h}_{tb}")
        consumed("kfm", f"tr_{h}_{TB-1}")
        consumed("vfm", f"tr_{h}_{TB-1}")
        if h == 0:
            kk = dbg_dump("kfm0", kfm, [128, N], BF16, f"kev_0_{NT-1}")
            if kk:
                consumed_add("kfm", kk)


        def kvev(e):
            e.tensor_copy(kvsb[:, :], pb[2][:, 0:128])
            return e.tensor_copy(ksumc[:, :], pb[2][:, 128:129])
        emit("vector", [f"kvmm_{h}_{TB-1}"], kvev, s_dve, 1, key=f"kvev_{h}")
        consumed("pb2", f"kvev_{h}")
        if h == 0:
            dbg_dump("kv", kvsb[:, :], [128, 128], BF16, "kvev_0")
            dbg_dump("kfm", kfm, [128, N], BF16, "kvev_0")
        for nt in range(NT):
            emit("tensor", [f"kvev_{h}"] + free_waits("pb3"),
                 lambda e, h=h, nt=nt: e.matmul(pb[3][0:1, 0:NTW], ksumc[:, :],
                                                bigB[:, sl(h * N, nt)], start=True, stop=True),
                 s_te, 1, key=f"zmm_{h}_{nt}")
            emit("vector", [f"zmm_{h}_{nt}", "epsd" if h == 0 and nt == 0 else None],
                 lambda e, nt=nt: e.tensor_tensor(rowC[0:1, nt * NTW:(nt + 1) * NTW],
                                                  pb[3][0:1, 0:NTW],
                                                  rowD[0:1, nt * NTW:(nt + 1) * NTW],
                                                  ALU.add),
                 s_dve, 1, key=f"zev_{h}_{nt}")
            consumed("pb3", f"zev_{h}_{nt}")
        for nt in range(NT):
            pn = 4 + nt % 2
            pz = 6 + nt % 2
            emit("tensor", [f"kvev_{h}"] + free_waits(f"pb{pn}"),
                 lambda e, h=h, nt=nt, pn=pn: e.matmul(pb[pn][:, 0:NTW], kvsb[:, :],
                                                       bigB[:, sl(h * N, nt)],
                                                       start=True, stop=True),
                 s_te, 1, key=f"nmm_{h}_{nt}")
            emit("tensor", [f"zev_{h}_{nt}"] + free_waits(f"pb{pz}"),
                 lambda e, nt=nt, pz=pz: e.matmul(pb[pz][:, 0:NTW], onesrow[:, :],
                                                  rowC[0:1, nt * NTW:(nt + 1) * NTW],
                                                  start=True, stop=True),
                 s_te, 1, key=f"zbmm_{h}_{nt}")
            emit("scalar", [f"zbmm_{h}_{nt}"] + free_waits(f"zbsb{nt % 2}"),
                 lambda e, nt=nt, pz=pz: e.activation(zbsb[nt % 2][:, 0:NTW],
                                                      pb[pz][:, 0:NTW], AF.Identity),
                 s_act, 1, key=f"zbev_{h}_{nt}")
            consumed(f"pb{pz}", f"zbev_{h}_{nt}")

            def prefn(e, h=h, nt=nt, pn=pn):
                e.reciprocal(zbsb[nt % 2][:, 0:NTW], zbsb[nt % 2][:, 0:NTW])
                return e.tensor_tensor(bigB[:, sl(h * N, nt)], zbsb[nt % 2][:, 0:NTW],
                                       pb[pn][:, 0:NTW], ALU.mult)
            emit("vector", [f"nmm_{h}_{nt}", f"zbev_{h}_{nt}"], prefn,
                 s_dve, 1, key=f"pre_{h}_{nt}")
            consumed(f"pb{pn}", f"pre_{h}_{nt}")
            consumed(f"zbsb{nt % 2}", f"pre_{h}_{nt}")
    EV["attn_done"] = EV[f"pre_{H-1}_{NT-1}"]
    k0 = dbg_dump("pre", bigB, [128, KC * N], BF16, "attn_done")
    if k0:
        FREE.setdefault("pb0", []).append(k0)

    for m in range(MCH):
        pp = m % 2
        emit("gpsimd", free_waits(f"wbuf{pp}"),
             lambda e, m=m, pp=pp: e.dma_start(out=wbuf[pp], in_=woT[m]),
             swb[pp], 16, key=f"ow{m}")
        base = 0 if pp == 0 else 4
        tw = [f"ow{m}", "attn_done"]
        for nt in range(NT):
            tw += free_waits(f"pb{base + nt}")

        def omm(e, m=m, pp=pp, base=base):
            last = None
            for k in range(KC):
                for nt in range(NT):
                    last = e.matmul(pb[base + nt][:, 0:NTW], wbuf[pp][:, 128 * k:128 * (k + 1)],
                                    bigB[:, sl(k * N, nt)], start=(k == 0), stop=(k == KC - 1))
            return last
        emit("tensor", tw, omm, s_te, 1, key=f"omm{m}")
        consumed(f"wbuf{pp}", f"omm{m}")

        def oev(e, m=m, base=base):
            last = None
            for nt in range(NT):
                last = e.activation(bigA[:, sl(m * N, nt)], pb[base + nt][:, 0:NTW],
                                    AF.Identity, bias=biaso[:, m:m + 1], scale=1.0)
            return last
        emit("scalar", [f"omm{m}"], oev, s_act, 1, key=f"oev{m}")
        for nt in range(NT):
            consumed(f"pb{base + nt}", f"oev{m}")
    EV["a_done"] = EV[f"oev{MCH-1}"]
    ka = dbg_dump("a", bigA, [128, KC * N], BF16, "a_done")
    if ka:
        FREE.setdefault("xb0", []).append(ka)

    ln_stats("ln1", "bigA", src_key_fn=lambda k: f"oev{k}")
    dbg_dump("mu1", rowA[:, :], [1, N], F32, "ln1_rows")
    dbg_dump("r1", rowB[:, :], [1, N], F32, "ln1_rows")
    ln_bcasts("ln1")
    for k in range(KC):
        pp = k % 2
        emit("gpsimd", free_waits(f"xb{pp}"),
             lambda e, k=k, pp=pp: e.dma_start(out=xb[pp][:, :], in_=xT[k]),
             sxg[pp], 16, key=f"x1_ld{k}")

        def x1fn(e, k=k, pp=pp):
            last = None
            for nt in range(NT):
                e.tensor_tensor(t32a[:, 0:NTW], bigA[:, sl(k * N, nt)], pb[nt][:, 0:NTW],
                                ALU.subtract)
                e.tensor_tensor(t32b[:, 0:NTW], t32a[:, 0:NTW], pb[4 + nt][:, 0:NTW], ALU.mult)
                last = e.tensor_tensor(bigB[:, sl(k * N, nt)], t32b[:, 0:NTW],
                                       xb[pp][:, nt * NTW:(nt + 1) * NTW], ALU.add)
            return last
        emit("vector", [f"x1_ld{k}", "ln1_bc", f"pre_{H-1}_{NT-1}" if k == 0 else None],
             x1fn, s_dve, 1, key=f"x1_k{k}")
        consumed(f"xb{pp}", f"x1_k{k}")
        emit("scalar", [f"x1_k{k}"],
             lambda e, k=k: e.dma_start(out=x1_sp[k], in_=bigB[:, k * N:(k + 1) * N]),
             spx, 16, key=f"x1sp{k}")
    EV["x1_done"] = EV[f"x1_k{KC-1}"]
    EV["x1_spilled"] = EV[f"x1sp{KC-1}"]
    for j in range(8):
        consumed(f"pb{j}", f"x1_k{KC-1}")

    ln_stats("ln2", "bigB", src_key_fn=lambda k: f"x1_k{k}")
    ln_bcasts("ln2")
    for k in range(KC):
        def x2fn(e, k=k):
            last = None
            for nt in range(NT):
                it = k * NT + nt
                e.tensor_tensor(t32a[:, 0:NTW], bigB[:, sl(k * N, nt)], pb[nt][:, 0:NTW],
                                ALU.subtract)
                e.tensor_tensor(bigA_h[:, sl(k * N, nt)], t32a[:, 0:NTW],
                                pb[4 + nt][:, 0:NTW], ALU.mult)
                last = e.reduce_max(xam[:, it:it + 1], bigA_h[:, sl(k * N, nt)], axis=AX.X,
                                    apply_absolute_value=True)
            return last
        emit("vector", ["ln2_bc"], x2fn, s_dve, 1, key=f"x2d_k{k}")
    EV["x2d_done"] = EV[f"x2d_k{KC-1}"]
    for j in range(8):
        consumed(f"pb{j}", f"x2d_k{KC-1}")

    def amred(e):
        e.reduce_max(amaxf[2][:, :], xam[:, 0:KC * NT], axis=AX.X)
        e.reduce_max(amaxf[0][:, :], wfam[:, :], axis=AX.X)
        return e.reduce_max(amaxf[1][:, :], wpam[:, :], axis=AX.X)
    emit("vector", ["x2d_done", "prepass"], amred, s_dve, 1, key="amred")

    def tr3(e):
        e.transpose(pb[0][0:1, 0:128], amaxf[2][:, :], idf[:, :])
        e.transpose(pb[1][0:1, 0:128], amaxf[0][:, :], idf[:, :])
        return e.transpose(pb[2][0:1, 0:128], amaxf[1][:, :], idf[:, :])
    emit("tensor", ["amred", "consts"] + free_waits("pb0")
         + free_waits("pb1") + free_waits("pb2"), tr3, s_te, 1, key="amx_tr")

    def amx(e):
        e.reduce_max(sc[0:1, S_XMAXL:S_XMAXL + 1], pb[0][0:1, 0:128], axis=AX.X)
        e.reduce_max(sc[0:1, S_WFMAX:S_WFMAX + 1], pb[1][0:1, 0:128], axis=AX.X)
        return e.reduce_max(sc[0:1, S_WPMAX:S_WPMAX + 1], pb[2][0:1, 0:128], axis=AX.X)
    emit("vector", ["amx_tr"], amx, s_dve, 1, key="amx_sc")
    consumed("pb0", "amx_sc")
    consumed("pb1", "amx_sc")
    consumed("pb2", "amx_sc")

    emit("gpsimd", ["amx_sc"], lambda e: e.dma_start(out=cc_in1.ap(), in_=sc[0:1, 0:16]),
         s_gld, 16, key="cc1_out")
    emit("gpsimd", ["cc1_out"],
         lambda e: e.collective_compute("AllReduce", ALU.max,
                                        replica_groups=[list(range(n_cores))],
                                        ins=[cc_in1.ap().opt()], outs=[cc_out1.ap().opt()]),
         s_cc, 1, key="cc1")
    emit("gpsimd", ["cc1"],
         lambda e: e.dma_start(out=sc[0:1, S_XMAXG:S_XMAXG + 1],
                               in_=cc_out1.ap()[0:1, S_XMAXL:S_XMAXL + 1]),
         s_gld, 16, key="cc1_back0")
    emit("gpsimd", ["cc1_back0"],
         lambda e: e.dma_start(out=sc[0:1, S_WFMAX:S_WPMAX + 1],
                               in_=cc_out1.ap()[0:1, S_WFMAX:S_WPMAX + 1]),
         s_gld, 16, key="cc1_back")

    def sc1(e):
        e.tensor_scalar(sc[0:1, S_SX:S_SX + 1], sc[0:1, S_XMAXG:S_XMAXG + 1],
                        1.0 / FP8_MAX, 1e-6, ALU.mult, ALU.max)
        e.tensor_scalar(sc[0:1, S_SW:S_SW + 1], sc[0:1, S_WFMAX:S_WFMAX + 1],
                        1.0 / FP8_MAX, 1e-6, ALU.mult, ALU.max)
        e.tensor_scalar(sc[0:1, S_SP:S_SP + 1], sc[0:1, S_WPMAX:S_WPMAX + 1],
                        1.0 / FP8_MAX, 1e-6, ALU.mult, ALU.max)
        e.reciprocal(sc[0:1, 0:1], sc[0:1, S_SX:S_SX + 1])
        e.reciprocal(sc[0:1, 1:2], sc[0:1, S_SW:S_SW + 1])
        e.reciprocal(sc[0:1, 2:3], sc[0:1, S_SP:S_SP + 1])
        e.scalar_tensor_tensor(sc[0:1, S_DQ1:S_DQ1 + 1], sc[0:1, S_SX:S_SX + 1], 4.0,
                               sc[0:1, S_SW:S_SW + 1], ALU.mult, ALU.mult)
        e.tensor_scalar_mul(sc[0:1, S_RSX2:S_RSX2 + 1], sc[0:1, 0:1], 0.5)
        e.tensor_scalar_mul(sc[0:1, S_RSW2:S_RSW2 + 1], sc[0:1, 1:2], 0.5)
        return e.tensor_scalar_mul(sc[0:1, S_RSP2:S_RSP2 + 1], sc[0:1, 2:3], 0.5)
    emit("vector", ["cc1_back"], sc1, s_dve, 1, key="sc1_math")
    emit("tensor", ["sc1_math"] + free_waits("pb3"),
         lambda e: e.matmul(pb[3][:, 0:NSC], onesrow[:, :], sc[0:1, 0:NSC],
                            start=True, stop=True),
         s_te, 1, key="scb1_mm")
    emit("vector", ["scb1_mm"],
         lambda e: e.tensor_copy(scb[:, :], pb[3][:, 0:NSC]),
         s_dve, 1, key="scb1")
    consumed("pb3", "scb1")

    for k in range(KC):
        if k % 2 == 0:
            def x8fn(e, k=k):
                last = None
                for nt in range(NT):
                    o = nt * (KC * NTW) + k * NTW
                    last = e.activation(x8v[:, o:o + NTW], bigA_h[:, sl(k * N, nt)],
                                        AF.Identity, bias=0.0,
                                        scale=scb[:, S_RSX2:S_RSX2 + 1])
                return last
            emit("scalar", ["scb1", "x1_spilled"], x8fn, s_act, 1, key=f"x8_k{k}")
        else:
            def x8fn(e, k=k):
                last = None
                for nt in range(NT):
                    o = nt * (KC * NTW) + k * NTW
                    last = e.tensor_scalar_mul(x8v[:, o:o + NTW], bigA_h[:, sl(k * N, nt)],
                                               scb[:, S_RSX2:S_RSX2 + 1])
                return last
            emit("vector", ["scb1", "x1_spilled"], x8fn, s_dve, 1, key=f"x8_k{k}")
    EV["x8_done"] = EV[f"x8_k{KC-1}"]

    def habs_emit(mm):
        pq = mm % 2

        def habs(e, mm=mm, pq=pq):
            return e.reduce_max(ham[:, mm:mm + 1], hbuf[pq][:, :], axis=AX.X,
                                apply_absolute_value=True)
        emit("vector", [f"hev{mm}"], habs, s_dve, 1, key=f"hmx{mm}")
        consumed_add(f"hbuf{pq}", f"hmx{mm}")

    for m in range(FC):
        pp = m % 2
        emit("sync", free_waits(f"wfbuf{pp}") + ([f"omm{MCH-1}"] if m < 2 else []),
             lambda e, m=m, pp=pp: e.dma_start(out=wfbuf[pp], in_=wfcT[m]),
             swf[pp], 16, key=f"wf_ld{m}")
        emit("vector", [f"wf_ld{m}", "scb1"] + free_waits(f"wf8b{pp}"),
             lambda e, pp=pp: e.tensor_scalar_mul(wf8b[pp], wfbuf[pp],
                                                  scb[:, S_RSW2:S_RSW2 + 1]),
             s_dve, 1, key=f"wf8_{m}")
        consumed(f"wfbuf{pp}", f"wf8_{m}")
        if m >= 2:
            habs_emit(m - 2)
        base = 0 if pp == 0 else 4

        def hmm_pair(e, c, pp, base):
            wap = wf8b[pp][:, 256 * c:256 * (c + 1)].rearrange('p (a b) -> p a b', a=2)
            last = None
            for nt in range(NT):
                o = nt * (KC * NTW) + 2 * c * NTW
                xap = x8v[:, o:o + 2 * NTW].rearrange('p (a b) -> p a b', a=2)
                last = e.matmul(pb[base + nt][:, 0:NTW], wap, xap,
                                start=(c == 0), stop=(c == KC // 2 - 1),
                                perf_mode=PM_DR)
            return last

        if m < 2:
            pbfree = []
            for nt in range(NT):
                pbfree += free_waits(f"pb{base + nt}")
            for c in range(KC // 2):
                tw = [f"wf8_{m}", f"x8_k{2*c}", f"x8_k{2*c+1}"] + (pbfree if c == 0 else [])
                emit("tensor", tw,
                     lambda e, c=c, pp=pp, base=base: hmm_pair(e, c, pp, base),
                     s_te, 1, key=f"hmm{m}_c{c}")
            EV[f"hmm{m}"] = EV[f"hmm{m}_c{KC//2-1}"]
        else:
            tw = [f"wf8_{m}", "x8_done"]
            for nt in range(NT):
                tw += free_waits(f"pb{base + nt}")

            def hmm(e, pp=pp, base=base):
                last = None
                for c in range(KC // 2):
                    last = hmm_pair(e, c, pp, base)
                return last
            emit("tensor", tw, hmm, s_te, 1, key=f"hmm{m}")
        consumed(f"wf8b{pp}", f"hmm{m}")

        def hev(e, pp=pp, base=base):
            for nt in range(NT):
                e.activation(hbuf[pp][:, nt * NTW:(nt + 1) * NTW],
                             pb[base + nt][:, 0:NTW], AF.Relu,
                             bias=0.0, scale=scb[:, S_DQ1:S_DQ1 + 1])
            return e.activation(hbuf[pp][:, :], hbuf[pp][:, :], AF.Square)
        emit("scalar", [f"hmm{m}"] + free_waits(f"hbuf{pp}"), hev, s_act, 1, key=f"hev{m}")
        for nt in range(NT):
            consumed(f"pb{base + nt}", f"hev{m}")

        emit("scalar", [f"hev{m}"],
             lambda e, m=m, pp=pp: e.dma_start(out=h_sp[m], in_=hbuf[pp]),
             sph[pp], 16, key=f"hsp{m}")
        consumed(f"hbuf{pp}", f"hsp{m}")
        if m == FC - 1:
            habs_emit(FC - 2)
            habs_emit(FC - 1)
    EV["hmax_done"] = EV[f"hmx{FC-1}"]
    dbg_dump("amax4", ham[:, :], [128, 64], F32, "hmax_done")
    dbg_dump("wfam", wfam[:, :], [128, 128], F32, "prepass")
    EV["mm1_sp0"] = EV[f"hsp{FC-2}"]
    EV["mm1_sp1"] = EV[f"hsp{FC-1}"]
    EV["mm1_mm_done"] = EV[f"hmm{FC-1}"]

    emit("vector", ["hmax_done"],
         lambda e: e.reduce_max(amaxf[3][:, :], ham[:, 0:FC], axis=AX.X),
         s_dve, 1, key="hamred")
    emit("tensor", ["hamred"] + free_waits("pb0"),
         lambda e: e.transpose(pb[0][0:1, 0:128], amaxf[3][:, :], idf[:, :]),
         s_te, 1, key="hmx_tr")
    emit("vector", ["hmx_tr"],
         lambda e: e.reduce_max(sc[0:1, S_HMAXL:S_HMAXL + 1], pb[0][0:1, 0:128], axis=AX.X),
         s_dve, 1, key="hmx_sc")
    consumed("pb0", "hmx_sc")
    emit("gpsimd", ["hmx_sc"], lambda e: e.dma_start(out=cc_in2.ap(), in_=sc[0:1, 0:16]),
         s_gld, 16, key="cc2_out")
    emit("gpsimd", ["cc2_out"],
         lambda e: e.collective_compute("AllReduce", ALU.max,
                                        replica_groups=[list(range(n_cores))],
                                        ins=[cc_in2.ap().opt()], outs=[cc_out2.ap().opt()]),
         s_cc, 1, key="cc2")
    emit("gpsimd", ["cc2"],
         lambda e: e.dma_start(out=sc[0:1, S_HMAXG:S_HMAXG + 1],
                               in_=cc_out2.ap()[0:1, S_HMAXL:S_HMAXL + 1]),
         s_gld, 16, key="cc2_back")

    def sc2(e):
        e.tensor_scalar(sc[0:1, S_SH:S_SH + 1], sc[0:1, S_HMAXG:S_HMAXG + 1],
                        1.0 / FP8_MAX, 1e-6, ALU.mult, ALU.max)
        e.tensor_copy(vsink[0:1, :], sc[0:1, S_SP:S_SP + 1])
        e.tensor_copy(vsink[0:1, :], sc[0:1, S_SP:S_SP + 1])
        e.reciprocal(sc[0:1, 1:2], sc[0:1, S_SH:S_SH + 1])
        e.scalar_tensor_tensor(sc[0:1, S_DQ2:S_DQ2 + 1], sc[0:1, S_SH:S_SH + 1], 4.0,
                               sc[0:1, S_SP:S_SP + 1], ALU.mult, ALU.mult)
        e.tensor_copy(vsink[0:1, :], sc[0:1, S_SP:S_SP + 1])
        e.tensor_copy(vsink[0:1, :], sc[0:1, S_SP:S_SP + 1])
        return e.tensor_scalar_mul(sc[0:1, S_SQS:S_SQS + 1], sc[0:1, 1:2], 0.5)
    emit("vector", ["cc2_back"], sc2, s_dve, 1, key="sc2_math")
    emit("tensor", ["sc2_math"] + free_waits("pb0"),
         lambda e: e.matmul(pb[0][:, 0:NSC], onesrow[:, :], sc[0:1, 0:NSC],
                            start=True, stop=True),
         s_te, 1, key="scb2_mm")
    emit("vector", ["scb2_mm"],
         lambda e: e.tensor_copy(scb[:, :], pb[0][:, 0:NSC]),
         s_dve, 1, key="scb2")
    consumed("pb0", "scb2")

    for m in range(FC):
        pp = m % 2
        emit("gpsimd", [f"hsp{m}"] + free_waits(f"xb{pp}"),
             lambda e, m=m, pp=pp: e.dma_start(out=xbh[pp], in_=h_sp[m]),
             sxb[pp], 16, key=f"hrd{m}")
        if m % 2 == 0:
            def h8fn(e, m=m, pp=pp):
                last = None
                for nt in range(NT):
                    o = nt * (FC * NTW) + m * NTW
                    last = e.activation(h8v[:, o:o + NTW],
                                        xbh[pp][:, nt * NTW:(nt + 1) * NTW],
                                        AF.Identity, bias=0.0,
                                        scale=scb[:, S_SQS:S_SQS + 1])
                return last
            emit("scalar", [f"hrd{m}", "scb2"], h8fn, s_act, 1, key=f"h8_{m}")
        else:
            def h8fn(e, m=m, pp=pp):
                last = None
                for nt in range(NT):
                    o = nt * (FC * NTW) + m * NTW
                    last = e.tensor_scalar_mul(h8v[:, o:o + NTW],
                                               xbh[pp][:, nt * NTW:(nt + 1) * NTW],
                                               scb[:, S_SQS:S_SQS + 1])
                return last
            emit("vector", [f"hrd{m}", "scb2"], h8fn, s_dve, 1, key=f"h8_{m}")
        consumed(f"xb{pp}", f"h8_{m}")
    EV["h8_done"] = EV[f"h8_{FC-1}"]

    subw2 = min(2048, FC * 128)
    nsub = (FC * 128) // subw2
    for m in range(MCH):
        mp = m % 2
        for j in range(nsub):
            i = m * nsub + j
            ip = i % 2
            emit("sync", free_waits(f"wpbuf{ip}") + ([f"wf8_{FC-1}"] if i < 2 else []),
                 lambda e, m=m, j=j, ip=ip: e.dma_start(out=wpbuf[ip][:, 0:subw2],
                                                        in_=wprT[m][:, j * subw2:(j + 1) * subw2]),
                 swp[ip], 16, key=f"wp_ld{i}")
            gw = ["mm1_mm_done", "mm1_sp0", "mm1_sp1"] if i < 2 * nsub else []
            emit("vector", [f"wp_ld{i}", "scb1"] + gw
                 + (free_waits(f"wp8b{mp}") if j == 0 else []),
                 lambda e, j=j, ip=ip, mp=mp: e.tensor_scalar_mul(
                     wp8b[mp][:, j * subw2:(j + 1) * subw2], wpbuf[ip][:, 0:subw2],
                     scb[:, S_RSP2:S_RSP2 + 1]),
                 s_dve, 1, key=f"wp8_{i}")
            consumed(f"wpbuf{ip}", f"wp8_{i}")
        base = 0 if mp == 0 else 4

        def o2mm_pair(e, c, mp, base):
            wap = wp8b[mp][:, 256 * c:256 * (c + 1)].rearrange('p (a b) -> p a b', a=2)
            last = None
            for nt in range(NT):
                o = nt * (FC * NTW) + 2 * c * NTW
                xap = h8v[:, o:o + 2 * NTW].rearrange('p (a b) -> p a b', a=2)
                last = e.matmul(pb[base + nt][:, 0:NTW], wap, xap,
                                start=(c == 0), stop=(c == FC // 2 - 1),
                                perf_mode=PM_DR)
            return last

        if m < 2:
            pbfree = []
            for nt in range(NT):
                pbfree += free_waits(f"pb{base + nt}")
            for c in range(FC // 2):
                tw = [f"wp8_{m * nsub + nsub - 1}", f"h8_{2*c}", f"h8_{2*c+1}"] \
                    + (pbfree if c == 0 else [])
                emit("tensor", tw,
                     lambda e, c=c, mp=mp, base=base: o2mm_pair(e, c, mp, base),
                     s_te, 1, key=f"omm2_{m}_c{c}")
            EV[f"omm2_{m}"] = EV[f"omm2_{m}_c{FC//2-1}"]
        else:
            tw = [f"wp8_{m * nsub + nsub - 1}", "h8_done"]
            for nt in range(NT):
                tw += free_waits(f"pb{base + nt}")

            def o2mm(e, mp=mp, base=base):
                last = None
                for c in range(FC // 2):
                    last = o2mm_pair(e, c, mp, base)
                return last
            emit("tensor", tw, o2mm, s_te, 1, key=f"omm2_{m}")
        consumed(f"wp8b{mp}", f"omm2_{m}")

        def o2ev(e, mp=mp, base=base):
            last = None
            for nt in range(NT):
                last = e.activation(obuf[mp][:, nt * NTW:(nt + 1) * NTW],
                                    pb[base + nt][:, 0:NTW], AF.Identity,
                                    bias=0.0, scale=scb[:, S_DQ2:S_DQ2 + 1])
            return last
        emit("scalar", [f"omm2_{m}"] + free_waits(f"obuf{mp}"), o2ev, s_act, 1, key=f"oev2_{m}")
        for nt in range(NT):
            consumed(f"pb{base + nt}", f"oev2_{m}")
        emit("scalar", [f"oev2_{m}"],
             lambda e, m=m, mp=mp: e.dma_start(out=o_sp[m], in_=obuf[mp]),
             spo[mp], 16, key=f"osp{m}")
        consumed(f"obuf{mp}", f"osp{m}")
    EV["mm2_sp0"] = EV[f"osp{MCH-2}"]
    EV["mm2_sp1"] = EV[f"osp{MCH-1}"]

    ln_stats("ln3", "osp")
    ln_bcasts("ln3")
    for k in range(KC):
        pp = k % 2
        emit("sync", [f"osp{k}"] + free_waits(f"xb{pp}"),
             lambda e, k=k, pp=pp: e.dma_start(out=xbh[pp], in_=o_sp[k]),
             sxb[pp], 16, key=f"ord{k}")
        emit("gpsimd", free_waits(f"x1b{pp}")
             + (["x1_spilled", f"omm2_{MCH-1}"] if k < 2 else []),
             lambda e, k=k, pp=pp: e.dma_start(out=x1b[pp], in_=x1_sp[k]),
             sx1b[pp], 16, key=f"x1rd{k}")

        def ofn(e, k=k, pp=pp):
            last = None
            for nt in range(NT):
                e.tensor_tensor(t32a[:, 0:NTW], xbh[pp][:, nt * NTW:(nt + 1) * NTW],
                                pb[nt][:, 0:NTW], ALU.subtract)
                e.tensor_tensor(t32b[:, 0:NTW], t32a[:, 0:NTW], pb[4 + nt][:, 0:NTW], ALU.mult)
                last = e.tensor_tensor(outb[pp][:, nt * NTW:(nt + 1) * NTW], t32b[:, 0:NTW],
                                       x1b[pp][:, nt * NTW:(nt + 1) * NTW], ALU.add)
            return last
        emit("vector", [f"ord{k}", f"x1rd{k}", "ln3_bc"] + free_waits(f"outb{pp}"),
             ofn, s_dve, 1, key=f"out_k{k}")
        consumed(f"xb{pp}", f"out_k{k}")
        consumed(f"x1b{pp}", f"out_k{k}")
        emit("sync" if k % 2 == 0 else "gpsimd", [f"out_k{k}"],
             lambda e, k=k, pp=pp: e.dma_start(out=outT[k], in_=outb[pp]),
             sout[pp], 16, key=f"st{k}")
        consumed(f"outb{pp}", f"st{k}")

    emit("sync", [f"st{KC-2}", f"st{KC-1}"], lambda e: e.nop())
    emit("tensor", [f"st{KC-2}", f"st{KC-1}"], lambda e: e.nop())

    def replay(eng_name):
        def fn(e):
            for wl, build, semx, n in recs[eng_name]:
                for s, v in wl:
                    e.wait_ge(s, v)
                instr = build(e)
                if semx is not None:
                    instr.then_inc(semx, n)
        return fn

    with nc.Block() as B:
        B.gpsimd(replay("gpsimd"))
        B.sync(replay("sync"))
        B.tensor(replay("tensor"))
        B.scalar(replay("scalar"))
        B.vector(replay("vector"))

    es.close()
    return nc


_D, _N, _H, _F, _NCORES = 2048, 2048, 16, 8192, 8
_nc_cache = {}


def _tile_wT(w):
    d_out, d_in = w.shape
    MCH, KC = d_out // 128, d_in // 128
    t = np.asarray(w, dtype=np.float32).reshape(MCH, 128, KC, 128)
    t = t.transpose(0, 3, 2, 1)
    return np.ascontiguousarray(t.reshape(MCH, 128, KC * 128))


def kernel(L0_attn_x, L0_attn_Wq, L0_attn_bq, L0_attn_Wk, L0_attn_bk,
           L0_attn_Wv, L0_attn_bv, L0_attn_Wo, L0_attn_bo,
           L0_ffn_w_fc, L0_ffn_w_proj):
    D, N, H, F, C = _D, _N, _H, _F, _NCORES
    KC = D // 128
    TB = N // 128

    x = np.ascontiguousarray(np.asarray(L0_attn_x, dtype=np.float32))
    assert x.shape == (C, N, D)

    if "nc" not in _nc_cache:
        _nc_cache["nc"] = build_nc(D, N, H, F, C)
    nc = _nc_cache["nc"]

    shared = dict(
        wqT=_tile_wT(L0_attn_Wq), wkT=_tile_wT(L0_attn_Wk), wvT=_tile_wT(L0_attn_Wv),
        woT=_tile_wT(L0_attn_Wo), wfcT=_tile_wT(L0_ffn_w_fc), wprT=_tile_wT(L0_ffn_w_proj),
    )

    def col(b):
        return np.ascontiguousarray(np.asarray(b, dtype=np.float32).reshape(KC, 128).T)

    pos = np.arange(N, dtype=np.float32)
    decay = np.exp(-pos / np.float32(N)).astype(np.float32)
    shared.update(
        bq_c=col(L0_attn_bq), bk_c=col(L0_attn_bk), bv_c=col(L0_attn_bv),
        bo_c=col(L0_attn_bo),
        identf=np.eye(128, dtype=np.float32),
        identb=np.eye(128, dtype=ml_dtypes.bfloat16),
        decrow_d=decay.reshape(1, N),
        deccol_d=np.ascontiguousarray(decay.reshape(TB, 128).T),
    )

    FC8 = (F // 128) // C
    M8 = (D // 128) // C
    in_maps = []
    for c in range(C):
        m = dict(shared)
        m["xT"] = np.ascontiguousarray(x[c].T).reshape(KC, 128, N)
        m["wfsc"] = shared["wfcT"][c * FC8:(c + 1) * FC8]
        m["wpsc"] = shared["wprT"][c * M8:(c + 1) * M8]
        in_maps.append(m)

    res = run_bass_kernel_spmd(nc, in_maps, core_ids=list(range(C)))
    out = np.empty((C, N, D), dtype=np.float32)
    for c in range(C):
        out[c] = res.results[c]["outT"].reshape(D, N).T
    return out


# version 14
# speedup vs baseline: 1.0839x; 1.0371x over previous
import numpy as np
import ml_dtypes

import concourse.bass as bass
import concourse.mybir as mybir
from concourse.bass_utils import run_bass_kernel_spmd

F32 = mybir.dt.float32
BF16 = mybir.dt.bfloat16
F16 = mybir.dt.float16
F8 = mybir.dt.float8e4
U8 = mybir.dt.uint8
AF = mybir.ActivationFunctionType
ALU = mybir.AluOpType
AX = mybir.AxisListType
PM_DR = mybir.MatmulPerfMode.DoubleRow

FP8_MAX = 448.0
EPS_ATTN = 1e-6
EPS_LN = 1e-5

S_XMAXL, S_WFMAX, S_WPMAX, S_HMAXL = 0, 1, 2, 3
S_XMAXG, S_HMAXG = 4, 5
S_SX, S_SW, S_SP, S_SH = 6, 7, 8, 9
S_RSX2, S_RSW2, S_RSP2, S_DQ1, S_SQS, S_DQ2 = 10, 11, 12, 13, 14, 15
NSC = 16

ENGS = ("gpsimd", "sync", "tensor", "scalar", "vector")


def build_nc(D, N, H, F, n_cores, debug=False):
    KC = D // 128
    MCH = D // 128
    FC = F // 128
    TB = N // 128
    NTW = min(512, N)
    NT = N // NTW
    assert H == KC and FC == 4 * KC

    nc = bass.Bass("TRN2", target_bir_lowering=False,
                   detect_race_conditions=False, dynamic_dma_scratch_size=16384)

    xT = nc.dram_tensor("xT", [KC, 128, N], F32, kind="ExternalInput").ap()
    wfsc = nc.dram_tensor("wfsc", [FC // n_cores, 128, KC * 128], F32,
                          kind="ExternalInput").ap()
    wpsc = nc.dram_tensor("wpsc", [MCH // n_cores, 128, FC * 128], F32,
                          kind="ExternalInput").ap()
    wqT = nc.dram_tensor("wqT", [MCH, 128, KC * 128], F32, kind="ExternalInput").ap()
    wkT = nc.dram_tensor("wkT", [MCH, 128, KC * 128], F32, kind="ExternalInput").ap()
    wvT = nc.dram_tensor("wvT", [MCH, 128, KC * 128], F32, kind="ExternalInput").ap()
    woT = nc.dram_tensor("woT", [MCH, 128, KC * 128], F32, kind="ExternalInput").ap()
    wfcT = nc.dram_tensor("wfcT", [FC, 128, KC * 128], F32, kind="ExternalInput").ap()
    wprT = nc.dram_tensor("wprT", [MCH, 128, FC * 128], F32, kind="ExternalInput").ap()
    bq_d = nc.dram_tensor("bq_c", [128, MCH], F32, kind="ExternalInput").ap()
    bk_d = nc.dram_tensor("bk_c", [128, MCH], F32, kind="ExternalInput").ap()
    bv_d = nc.dram_tensor("bv_c", [128, MCH], F32, kind="ExternalInput").ap()
    bo_d = nc.dram_tensor("bo_c", [128, MCH], F32, kind="ExternalInput").ap()
    identf = nc.dram_tensor("identf", [128, 128], F32, kind="ExternalInput").ap()
    identb = nc.dram_tensor("identb", [128, 128], BF16, kind="ExternalInput").ap()
    decrow_d = nc.dram_tensor("decrow_d", [1, N], F32, kind="ExternalInput").ap()
    deccol_d = nc.dram_tensor("deccol_d", [128, TB], F32, kind="ExternalInput").ap()
    outT = nc.dram_tensor("outT", [KC, 128, N], F32, kind="ExternalOutput").ap()

    dbg = dict(kind="ExternalOutput") if debug else {}
    h_sp = nc.dram_tensor("h_sp", [FC, 128, N], F16, **dbg).ap()
    x1_sp = nc.dram_tensor("x1_sp", [KC, 128, N], BF16, **dbg).ap()
    o_sp = nc.dram_tensor("o_sp", [KC, 128, N], F16, **dbg).ap()
    sc_dram = nc.dram_tensor("sc_dram", [1, NSC], F32, **dbg).ap()
    cc_in1 = nc.dram_tensor("cc_in1", [1, 16], F32)
    cc_out1 = nc.dram_tensor("cc_out1", [1, 16], F32)
    cc_in2 = nc.dram_tensor("cc_in2", [1, 16], F32)
    cc_out2 = nc.dram_tensor("cc_out2", [1, 16], F32)

    from contextlib import ExitStack
    es = ExitStack()
    sb = lambda name, shape, dt: es.enter_context(nc.sbuf_tensor(name, shape, dt))
    psm = lambda name, shape: es.enter_context(nc.psum_tensor(name, shape, F32))
    mksem = lambda name: es.enter_context(nc.semaphore(name))

    mega = sb("mega", [128, 4 * KC * N], U8)
    bigA = mega[:, 0:KC * N * 2].bitcast(BF16)
    bigA_h = mega[:, 0:KC * N * 2].bitcast(F16)
    bigB = mega[:, KC * N * 2:4 * KC * N].bitcast(BF16)
    x8v = mega[:, KC * N * 2:KC * N * 3].bitcast(F8)
    h8v = mega[:, :].bitcast(F8)

    M2B = 48 * 1024
    mega2 = sb("mega2", [128, M2B], U8)

    def m2(lo, nbytes, dt):
        assert lo + nbytes <= M2B, (lo, nbytes)
        return mega2[:, lo:lo + nbytes].bitcast(dt)

    wbuf = [m2(0, KC * 256, BF16), m2(KC * 256, KC * 256, BF16)]
    _o2 = KC * 512
    kfm = m2(_o2, 2 * N, BF16)
    vfm = m2(_o2 + 2 * N, 2 * N, BF16)
    _o3 = _o2 + 4 * N
    ktok = [m2(_o3 + i * 512, 256, BF16) for i in range(4)]
    vtok = [m2(_o3 + i * 512 + 256, 256, BF16) for i in range(4)]
    kvsb = m2(_o3 + 2048, 256, BF16)
    ksumc = m2(_o3 + 2304, 2, BF16)
    _o4 = _o3 + 2560
    zbsb = [m2(_o4, NTW * 4, F32), m2(_o4 + NTW * 4, NTW * 4, F32)]
    _o5 = _o4 + NTW * 8
    pre = [m2(_o5, 4096, F32), m2(_o5 + 4096, 4096, F32)]
    _o6 = _o5 + 8192
    rowC = m2(_o6, 4 * N, F32)[0:1, :]
    rowD = m2(_o6 + 4 * N, 4 * N, F32)[0:1, :]
    assert _o6 + 8 * N <= M2B
    wfbuf = [m2(0, KC * 512, F32), m2(KC * 512, KC * 512, F32)]
    _f1 = KC * 1024
    wf8b = [m2(_f1, KC * 128, F8), m2(_f1 + KC * 128, KC * 128, F8)]
    _f2 = _f1 + 2 * KC * 128
    hbuf = [m2(_f2, 2 * N, F16), m2(_f2 + 2 * N, 2 * N, F16)]
    wpbuf = [m2(0, 8192, F32), m2(8192, 8192, F32)]
    wp8b = [m2(16384, FC * 128, F8), m2(16384 + FC * 128, FC * 128, F8)]
    _m2o = 16384 + 2 * FC * 128
    obuf = [m2(_m2o, 2 * N, F16), m2(_m2o + 2 * N, 2 * N, F16)]
    outb = [m2(0, 4 * N, F32), m2(4 * N, 4 * N, F32)]
    x1b = [m2(8 * N, 2 * N, BF16), m2(10 * N, 2 * N, BF16)]
    mubf = m2(16 * N, 2 * N, BF16)
    rbf = m2(18 * N, 2 * N, BF16)
    muh = mega[:, 0:2 * N].bitcast(F16)
    rh = mega[:, 2 * N:4 * N].bitcast(F16)
    _sq = M2B - 4 * N
    sqb = [m2(_sq, 2 * N, BF16), m2(_sq + 2 * N, 2 * N, BF16)]
    sqh = [m2(_sq, 2 * N, F16), m2(_sq + 2 * N, 2 * N, F16)]
    assert _sq >= 12 * N and _sq >= _m2o + 4 * N and _sq >= _f2 + 4 * N
    assert _sq >= _o6 + 4 * N

    xb = [sb("xb0", [128, N], BF16), sb("xb1", [128, N], BF16)]
    xbh = [xb[0][:, :].bitcast(F16), xb[1][:, :].bitcast(F16)]
    t32a = sb("t32a", [128, NTW], F32)
    t32b = sb("t32b", [128, NTW], F32)
    rowA = sb("rowA", [1, N], F32)
    rowB = sb("rowB", [1, N], F32)
    idf = sb("idf", [128, 128], F32)
    idb = sb("idb", [128, 128], BF16)
    onesrow = sb("onesrow", [1, 128], F32)
    ones1f = sb("ones1f", [1, 1], F32)
    onescol = sb("onescol", [128, 1], BF16)
    onescol_h = sb("onescol_h", [128, 1], F16)
    sc = sb("sc", [1, NSC], F32)
    scb = sb("scb", [128, NSC], F32)
    wfam = sb("wfam", [128, 128], F32)
    wpam = sb("wpam", [128, 128], F32)
    xam = sb("xam", [128, 64], F32)
    ham = sb("ham", [128, 64], F32)
    amaxf = [sb(f"amaxf{i}", [128, 1], F32) for i in range(4)]
    prem = [sb("prem0", [128, 1], F32), sb("prem1", [128, 1], F32)]
    vsink = sb("vsink", [128, 1], F32)
    asink = sb("asink", [128, 1], F32)
    biasq = sb("biasq", [128, MCH], F32)
    biask = sb("biask", [128, MCH], F32)
    biasv = sb("biasv", [128, MCH], F32)
    biaso = sb("biaso", [128, MCH], F32)
    deccol = sb("deccol", [128, TB], F32)

    pb = [psm(f"pb{j}", [128, 512]) for j in range(8)]
    pbb = {j: pb[j][:, :].bitcast(BF16) for j in (0, 1, 6, 7)}

    s_gld = mksem("s_gld")
    s_te = mksem("s_te")
    s_dve = mksem("s_dve")
    s_act = mksem("s_act")
    s_cc = mksem("s_cc")
    sconst = mksem("sconst")
    sxb = [mksem("sxb0"), mksem("sxb1")]
    sxg = [mksem("sxg0"), mksem("sxg1")]
    swb = [mksem("swb0"), mksem("swb1")]
    swf = [mksem("swf0"), mksem("swf1")]
    swp = [mksem("swp0"), mksem("swp1")]
    spre = [mksem("spre0"), mksem("spre1")]
    sx1b = [mksem("sx1b0"), mksem("sx1b1")]
    sph = [mksem("sph0"), mksem("sph1")]
    spo = [mksem("spo0"), mksem("spo1")]
    spx = mksem("spx")
    sout = [mksem("sout0"), mksem("sout1")]
    ALLSEMS = [s_gld, s_te, s_dve, s_act, s_cc, sconst, spx] + \
        sxb + sxg + swb + swf + swp + spre + sx1b + sph + spo + sout

    recs = {e: [] for e in ENGS}
    CNT = {s.name: 0 for s in ALLSEMS}
    EV = {}
    READY = {}
    FREE = {}

    def emit(eng, waits, build, sem=None, n=1, key=None):
        wl = []
        seen = set()
        for w in waits:
            if w is None:
                continue
            s, v = EV[w] if isinstance(w, str) else w
            if (s.name, v) in seen:
                continue
            seen.add((s.name, v))
            wl.append((s, v))
        if sem is not None:
            CNT[sem.name] += n
            val = CNT[sem.name]
            if key is not None:
                EV[key] = (sem, val)
            recs[eng].append((wl, build, sem, n))
            return key
        recs[eng].append((wl, build, None, 0))
        return None

    def produce(res, key):
        READY[res] = key

    def ready(res):
        return READY.get(res)

    def consumed(res, key):
        FREE[res] = [key]

    def consumed_add(res, key):
        FREE.setdefault(res, []).append(key)

    def free_waits(res):
        w = FREE.get(res, [])
        FREE[res] = []
        return w

    def sl(base, nt):
        return slice(base + nt * NTW, base + (nt + 1) * NTW)

    DBG_BUFS = {}

    def dbg_dump(tag, src_ap, shape, dt, wait_key):
        if debug != 2:
            return
        t = nc.dram_tensor(f"d_{tag}", shape, dt, kind="ExternalOutput").ap()
        DBG_BUFS[tag] = t
        key = f"dbg_{tag}"
        emit("gpsimd", [wait_key],
             lambda e: e.dma_start(out=t, in_=src_ap, allow_slow_non_contiguous=True),
             s_gld, 16, key=key)
        return key

    const_pairs = [(idf[:, :], identf), (idb[:, :], identb), (biasq[:, :], bq_d),
                   (biask[:, :], bk_d), (biasv[:, :], bv_d), (biaso[:, :], bo_d),
                   (deccol[:, 0:TB], deccol_d)]
    for i, (dst, src) in enumerate(const_pairs):
        emit("gpsimd", [], lambda e, dst=dst, src=src: e.dma_start(out=dst, in_=src),
             sconst, 16, key=f"c{i}")
    EV["consts"] = EV[f"c{len(const_pairs)-1}"]
    emit("vector", [], lambda e: e.memset(onesrow[:, :], 1.0))
    emit("vector", [], lambda e: e.memset(ones1f[:, :], 1.0))
    emit("vector", [], lambda e: e.memset(onescol[:, :], 1.0))
    emit("vector", [], lambda e: e.memset(onescol_h[:, :], 1.0))
    emit("vector", [], lambda e: e.memset(sc[:, :], 0.0))
    emit("vector", [], lambda e: e.memset(wfam[:, :], 0.0))
    emit("vector", [], lambda e: e.memset(wpam[:, :], 0.0))
    emit("vector", [], lambda e: e.memset(xam[:, :], 0.0))
    emit("vector", [], lambda e: e.memset(ham[:, :], 0.0))
    emit("vector", [], lambda e: e.memset(scb[:, :], 0.0), s_dve, 1, key="ones_done")
    for _e in ("sync", "tensor", "scalar", "vector"):
        emit(_e, ["consts", "ones_done"], lambda e: e.nop())

    def ln_stats(tag, kind, src_ready_key=None, src_key_fn=None):
        f16src = kind == "osp"
        oc = onescol_h if f16src else onescol
        for k in range(KC):
            pp = k % 2
            if kind == "x0":
                emit("gpsimd", free_waits(f"xb{pp}"),
                     lambda e, k=k, pp=pp: e.dma_start(out=xb[pp][:, :], in_=xT[k]),
                     sxg[pp], 16, key=f"{tag}_ld{k}")
                srcap = xb[pp][:, :]
                rk = f"{tag}_ld{k}"
            elif kind == "osp":
                emit("sync", [f"osp{k}"] + free_waits(f"xb{pp}"),
                     lambda e, k=k, pp=pp: e.dma_start(out=xbh[pp], in_=o_sp[k]),
                     sxb[pp], 16, key=f"{tag}_ld{k}")
                srcap = xbh[pp]
                rk = f"{tag}_ld{k}"
            else:
                srcap = (bigA if kind == "bigA" else bigB)[:, k * N:(k + 1) * N]
                rk = src_key_fn(k) if src_key_fn else src_ready_key
            sq = sqh[pp] if f16src else sqb[pp]
            hw = N // 2
            sqfree = free_waits(f"sq{pp}")
            emit("scalar", [rk] + sqfree,
                 lambda e, srcap=srcap, sq=sq, hw=hw:
                     e.activation(sq[:, 0:hw], srcap[:, 0:hw], AF.Square),
                 s_act, 1, key=f"{tag}_sqa{k}")
            if f16src:
                def sqv(e, srcap=srcap, sq=sq, hw=hw):
                    return e.tensor_mul(sq[:, hw:N], srcap[:, hw:N], srcap[:, hw:N])
            else:
                def sqv(e, srcap=srcap, sq=sq, hw=hw):
                    return e.tensor_mul(sq[:, hw:N], srcap[:, hw:N], srcap[:, hw:N])
            emit("vector", [rk] + sqfree, sqv, s_dve, 1, key=f"{tag}_sqb{k}")
            produce(f"sq{pp}", f"{tag}_sqa{k}")
            tw = [rk, f"{tag}_sqa{k}", f"{tag}_sqb{k}", "ones_done"]
            if k == 0:
                for j in range(8):
                    tw += free_waits(f"pb{j}")

            def mmfn(e, k=k, srcap=srcap, sq=sq, oc=oc):
                last = None
                for nt in range(NT):
                    e.matmul(pb[nt][0:1, 0:NTW], oc[:, :],
                             srcap[:, nt * NTW:(nt + 1) * NTW],
                             start=(k == 0), stop=(k == KC - 1))
                    last = e.matmul(pb[4 + nt][0:1, 0:NTW], oc[:, :],
                                    sq[:, nt * NTW:(nt + 1) * NTW],
                                    start=(k == 0), stop=(k == KC - 1))
                return last
            emit("tensor", tw, mmfn, s_te, 1, key=f"{tag}_mm{k}")
            consumed(f"sq{pp}", f"{tag}_mm{k}")
            if kind in ("x0", "osp"):
                consumed(f"xb{pp}", f"{tag}_mm{k}")

        def rowsfn(e):
            for nt in range(NT):
                e.tensor_copy(rowA[0:1, nt * NTW:(nt + 1) * NTW], pb[nt][0:1, 0:NTW])
                e.tensor_copy(rowB[0:1, nt * NTW:(nt + 1) * NTW], pb[4 + nt][0:1, 0:NTW])
            e.tensor_scalar_mul(rowA[:, :], rowA[:, :], 1.0 / D)
            e.tensor_scalar_mul(rowB[:, :], rowB[:, :], 1.0 / D)
            e.tensor_mul(rowC, rowA[:, :], rowA[:, :])
            e.tensor_sub(rowB[:, :], rowB[:, :], rowC)
            e.tensor_scalar_add(rowB[:, :], rowB[:, :], EPS_LN)
            return e.tensor_copy(vsink[0:1, :], rowB[0:1, 0:1])
        emit("vector", [f"{tag}_mm{KC-1}"], rowsfn, s_dve, 1, key=f"{tag}_var")
        for j in range(8):
            consumed(f"pb{j}", f"{tag}_var")
        def sqrtfn(e):
            e.activation(rowB[:, :], rowB[:, :], AF.Sqrt)
            return e.copy(asink[0:1, :], rowB[0:1, 0:1])
        emit("scalar", [f"{tag}_var"], sqrtfn, s_act, 1, key=f"{tag}_sqrt")
        def recfn(e):
            e.reciprocal(rowB[:, :], rowB[:, :])
            return e.tensor_copy(vsink[0:1, :], rowB[0:1, 0:1])
        emit("vector", [f"{tag}_sqrt"], recfn, s_dve, 1, key=f"{tag}_rows")

    def ln_bcasts(tag):
        w = [f"{tag}_rows"]
        for j in range(8):
            w += free_waits(f"pb{j}")

        def bfn(e):
            last = None
            for nt in range(NT):
                e.matmul(pb[nt][:, 0:NTW], onesrow[:, :],
                         rowA[0:1, nt * NTW:(nt + 1) * NTW], start=True, stop=True)
                last = e.matmul(pb[4 + nt][:, 0:NTW], onesrow[:, :],
                                rowB[0:1, nt * NTW:(nt + 1) * NTW], start=True, stop=True)
            return last
        emit("tensor", w, bfn, s_te, 1, key=f"{tag}_bc")

    ln_stats("ln0", "x0")
    emit("gpsimd", [f"ln0_mm{KC-1}"], lambda e: e.dma_start(out=rowD, in_=decrow_d),
         sconst, 16, key="ld_decrow")

    def epsdfn(e):
        e.reciprocal(rowD, rowD)
        return e.tensor_scalar_mul(rowD, rowD, EPS_ATTN)
    emit("vector", ["ld_decrow"], epsdfn, s_dve, 1, key="epsd")
    ln_bcasts("ln0")
    for k in range(KC):
        pp = k % 2
        emit("gpsimd", free_waits(f"xb{pp}"),
             lambda e, k=k, pp=pp: e.dma_start(out=xb[pp][:, :], in_=xT[k]),
             sxg[pp], 16, key=f"ln0a_ld{k}")

        def apfn(e, k=k, pp=pp):
            last = None
            for nt in range(NT):
                e.tensor_tensor(t32a[:, 0:NTW], xb[pp][:, nt * NTW:(nt + 1) * NTW],
                                pb[nt][:, 0:NTW], ALU.subtract)
                last = e.tensor_tensor(bigA[:, sl(k * N, nt)], t32a[:, 0:NTW],
                                       pb[4 + nt][:, 0:NTW], ALU.mult)
            return last
        emit("vector", [f"ln0a_ld{k}", "ln0_bc"], apfn, s_dve, 1, key=f"ln0a_k{k}")
        consumed(f"xb{pp}", f"ln0a_k{k}")
    EV["ln0_done"] = EV[f"ln0a_k{KC-1}"]
    for j in range(8):
        consumed(f"pb{j}", f"ln0a_k{KC-1}")

    def prepass():
        sub = 1024
        items = [("prf", wfsc, FC // n_cores, (KC * 128) // sub, wfam),
                 ("prp", wpsc, MCH // n_cores, (FC * 128) // sub, wpam)]
        idx = 0
        for tg, src, M, nsub, acc in items:
            assert M * nsub <= 64
            for i in range(M * nsub):
                m, j = divmod(i, nsub)
                pp = idx % 2
                idx += 1
                emit("sync", free_waits(f"pre{pp}"),
                     lambda e, m=m, j=j, pp=pp, src=src:
                         e.dma_start(out=pre[pp][:, 0:sub], in_=src[m][:, j * sub:(j + 1) * sub]),
                     spre[pp], 16, key=f"{tg}_ld{i}")

                def rfn(e, i=i, pp=pp, acc=acc):
                    return e.reduce_max(acc[:, i:i + 1], pre[pp][:, 0:sub], axis=AX.X,
                                        apply_absolute_value=True)
                emit("vector", [f"{tg}_ld{i}"], rfn, s_dve, 1, key=f"{tg}_{i}")
                consumed(f"pre{pp}", f"{tg}_{i}")
        EV["prepass"] = EV[f"prp_{(MCH // n_cores) * ((FC * 128) // sub) - 1}"]

    prepass()


    for m in range(MCH):
        pp = m % 2
        emit("gpsimd", free_waits(f"wbuf{pp}"),
             lambda e, m=m, pp=pp: e.dma_start(out=wbuf[pp], in_=wqT[m]),
             swb[pp], 16, key=f"qw{m}")
        base = 0 if pp == 0 else 4

        def qmm_k(e, k, pp, base):
            last = None
            for nt in range(NT):
                last = e.matmul(pb[base + nt][:, 0:NTW], wbuf[pp][:, 128 * k:128 * (k + 1)],
                                bigA[:, sl(k * N, nt)], start=(k == 0), stop=(k == KC - 1))
            return last

        if m == 0:
            pbfree = []
            for nt in range(NT):
                pbfree += free_waits(f"pb{base + nt}")
            for k in range(KC):
                tw = [f"qw{m}", f"ln0a_k{k}"] + (pbfree if k == 0 else [])
                emit("tensor", tw, lambda e, k=k, pp=pp, base=base: qmm_k(e, k, pp, base),
                     s_te, 1, key=f"qmm{m}_k{k}")
            EV[f"qmm{m}"] = EV[f"qmm{m}_k{KC-1}"]
        else:
            tw = [f"qw{m}", "ln0_done"]
            for nt in range(NT):
                tw += free_waits(f"pb{base + nt}")

            def qmm(e, m=m, pp=pp, base=base):
                last = None
                for k in range(KC):
                    last = qmm_k(e, k, pp, base)
                return last
            emit("tensor", tw, qmm, s_te, 1, key=f"qmm{m}")
        consumed(f"wbuf{pp}", f"qmm{m}")

        def qev(e, m=m, base=base):
            last = None
            for nt in range(NT):
                last = e.activation(bigB[:, sl(m * N, nt)], pb[base + nt][:, 0:NTW],
                                    AF.Relu, bias=biasq[:, m:m + 1], scale=1.0)
            return last
        emit("scalar", [f"qmm{m}"], qev, s_act, 1, key=f"qev{m}")
        for nt in range(NT):
            consumed(f"pb{base + nt}", f"qev{m}")
    EV["q_done"] = EV[f"qev{MCH-1}"]
    dbg_dump("q", bigB, [128, KC * N], BF16, "q_done")

    for h in range(H):
        emit("gpsimd", free_waits("wbuf0"),
             lambda e, h=h: e.dma_start(out=wbuf[0], in_=wkT[h]), swb[0], 16, key=f"wk{h}")
        emit("gpsimd", free_waits("wbuf1"),
             lambda e, h=h: e.dma_start(out=wbuf[1], in_=wvT[h]), swb[1], 16, key=f"wv{h}")
        for which, wkey, outt, biast, func, base in (
                ("k", f"wk{h}", kfm, biask, AF.Relu, 0),
                ("v", f"wv{h}", vfm, biasv, AF.Identity, 4)):
            wb = wbuf[0] if which == "k" else wbuf[1]
            tw = [wkey] + free_waits(which + "fm")
            for nt in range(NT):
                tw += free_waits(f"pb{base + nt}")

            def pmm(e, wb=wb, base=base):
                last = None
                for k in range(KC):
                    for nt in range(NT):
                        last = e.matmul(pb[base + nt][:, 0:NTW], wb[:, 128 * k:128 * (k + 1)],
                                        bigA[:, sl(k * N, nt)], start=(k == 0), stop=(k == KC - 1))
                return last
            emit("tensor", tw, pmm, s_te, 1, key=f"{which}mm_{h}")
            for nt in range(NT):
                emit("scalar", [f"{which}mm_{h}"],
                     lambda e, nt=nt, base=base, outt=outt, biast=biast, func=func, h=h:
                         e.activation(outt[:, nt * NTW:(nt + 1) * NTW], pb[base + nt][:, 0:NTW],
                                      func, bias=biast[:, h:h + 1], scale=1.0),
                     s_act, 1, key=f"{which}ev_{h}_{nt}")
                consumed(f"pb{base + nt}", f"{which}ev_{h}_{nt}")
            consumed(f"wbuf{0 if which == 'k' else 1}", f"{which}mm_{h}")
        for tb in range(TB):
            bnk = (0, 1, 6, 7)[tb % 4]
            sl4 = tb % 4
            tw = [f"kev_{h}_{NT-1}", f"vev_{h}_{NT-1}"] + free_waits(f"pb{bnk}")
            if tb == 0:
                tw += free_waits("pb2")

            def trfn(e, tb=tb, bnk=bnk):
                e.transpose(pbb[bnk][:, 0:128], kfm[:, 128 * tb:128 * (tb + 1)], idb[:, :])
                return e.transpose(pbb[bnk][:, 128:256], vfm[:, 128 * tb:128 * (tb + 1)], idb[:, :])
            emit("tensor", tw, trfn, s_te, 1, key=f"tr_{h}_{tb}")

            def tokfn(e, tb=tb, bnk=bnk, sl4=sl4):
                e.tensor_scalar_mul(ktok[sl4][:, :], pbb[bnk][:, 0:128], deccol[:, tb:tb + 1])
                return e.tensor_copy(vtok[sl4][:, :], pbb[bnk][:, 128:256])
            emit("vector", [f"tr_{h}_{tb}", "consts"] + free_waits(f"ktok{sl4}"),
                 tokfn, s_dve, 1, key=f"tok_{h}_{tb}")
            consumed(f"pb{bnk}", f"tok_{h}_{tb}")

            def kvfn(e, tb=tb, sl4=sl4):
                e.matmul(pb[2][:, 0:128], ktok[sl4][:, :], vtok[sl4][:, :],
                         start=(tb == 0), stop=False)
                return e.matmul(pb[2][:, 128:129], ktok[sl4][:, :], onescol[:, :],
                                start=False, stop=(tb == TB - 1))
            emit("tensor", [f"tok_{h}_{tb}"], kvfn, s_te, 1,
                 key=f"kvmm_{h}_{tb}")
            consumed(f"ktok{sl4}", f"kvmm_{h}_{tb}")
        consumed("kfm", f"tr_{h}_{TB-1}")
        consumed("vfm", f"tr_{h}_{TB-1}")
        if h == 0:
            kk = dbg_dump("kfm0", kfm, [128, N], BF16, f"kev_0_{NT-1}")
            if kk:
                consumed_add("kfm", kk)


        def kvev(e):
            e.tensor_copy(kvsb[:, :], pb[2][:, 0:128])
            return e.tensor_copy(ksumc[:, :], pb[2][:, 128:129])
        emit("vector", [f"kvmm_{h}_{TB-1}"], kvev, s_dve, 1, key=f"kvev_{h}")
        consumed("pb2", f"kvev_{h}")
        if h == 0:
            dbg_dump("kv", kvsb[:, :], [128, 128], BF16, "kvev_0")
            dbg_dump("kfm", kfm, [128, N], BF16, "kvev_0")
        for nt in range(NT):
            emit("tensor", [f"kvev_{h}"] + free_waits("pb3"),
                 lambda e, h=h, nt=nt: e.matmul(pb[3][0:1, 0:NTW], ksumc[:, :],
                                                bigB[:, sl(h * N, nt)], start=True, stop=True),
                 s_te, 1, key=f"zmm_{h}_{nt}")
            emit("vector", [f"zmm_{h}_{nt}", "epsd" if h == 0 and nt == 0 else None],
                 lambda e, nt=nt: e.tensor_tensor(rowC[0:1, nt * NTW:(nt + 1) * NTW],
                                                  pb[3][0:1, 0:NTW],
                                                  rowD[0:1, nt * NTW:(nt + 1) * NTW],
                                                  ALU.add),
                 s_dve, 1, key=f"zev_{h}_{nt}")
            consumed("pb3", f"zev_{h}_{nt}")
        for nt in range(NT):
            pn = 4 + nt % 2
            pz = 6 + nt % 2
            emit("tensor", [f"kvev_{h}"] + free_waits(f"pb{pn}"),
                 lambda e, h=h, nt=nt, pn=pn: e.matmul(pb[pn][:, 0:NTW], kvsb[:, :],
                                                       bigB[:, sl(h * N, nt)],
                                                       start=True, stop=True),
                 s_te, 1, key=f"nmm_{h}_{nt}")
            emit("tensor", [f"zev_{h}_{nt}"] + free_waits(f"pb{pz}"),
                 lambda e, nt=nt, pz=pz: e.matmul(pb[pz][:, 0:NTW], onesrow[:, :],
                                                  rowC[0:1, nt * NTW:(nt + 1) * NTW],
                                                  start=True, stop=True),
                 s_te, 1, key=f"zbmm_{h}_{nt}")
            emit("scalar", [f"zbmm_{h}_{nt}"] + free_waits(f"zbsb{nt % 2}"),
                 lambda e, nt=nt, pz=pz: e.activation(zbsb[nt % 2][:, 0:NTW],
                                                      pb[pz][:, 0:NTW], AF.Identity),
                 s_act, 1, key=f"zbev_{h}_{nt}")
            consumed(f"pb{pz}", f"zbev_{h}_{nt}")

            def prefn(e, h=h, nt=nt, pn=pn):
                e.reciprocal(zbsb[nt % 2][:, 0:NTW], zbsb[nt % 2][:, 0:NTW])
                return e.tensor_tensor(bigB[:, sl(h * N, nt)], zbsb[nt % 2][:, 0:NTW],
                                       pb[pn][:, 0:NTW], ALU.mult)
            emit("vector", [f"nmm_{h}_{nt}", f"zbev_{h}_{nt}"], prefn,
                 s_dve, 1, key=f"pre_{h}_{nt}")
            consumed(f"pb{pn}", f"pre_{h}_{nt}")
            consumed(f"zbsb{nt % 2}", f"pre_{h}_{nt}")
    EV["attn_done"] = EV[f"pre_{H-1}_{NT-1}"]
    k0 = dbg_dump("pre", bigB, [128, KC * N], BF16, "attn_done")
    if k0:
        FREE.setdefault("pb0", []).append(k0)

    for m in range(MCH):
        pp = m % 2
        emit("gpsimd", free_waits(f"wbuf{pp}"),
             lambda e, m=m, pp=pp: e.dma_start(out=wbuf[pp], in_=woT[m]),
             swb[pp], 16, key=f"ow{m}")
        base = 0 if pp == 0 else 4
        tw = [f"ow{m}", "attn_done"]
        for nt in range(NT):
            tw += free_waits(f"pb{base + nt}")

        def omm(e, m=m, pp=pp, base=base):
            last = None
            for k in range(KC):
                for nt in range(NT):
                    last = e.matmul(pb[base + nt][:, 0:NTW], wbuf[pp][:, 128 * k:128 * (k + 1)],
                                    bigB[:, sl(k * N, nt)], start=(k == 0), stop=(k == KC - 1))
            return last
        emit("tensor", tw, omm, s_te, 1, key=f"omm{m}")
        consumed(f"wbuf{pp}", f"omm{m}")

        def oev(e, m=m, base=base):
            last = None
            for nt in range(NT):
                last = e.activation(bigA[:, sl(m * N, nt)], pb[base + nt][:, 0:NTW],
                                    AF.Identity, bias=biaso[:, m:m + 1], scale=1.0)
            return last
        emit("scalar", [f"omm{m}"], oev, s_act, 1, key=f"oev{m}")
        for nt in range(NT):
            consumed(f"pb{base + nt}", f"oev{m}")
    EV["a_done"] = EV[f"oev{MCH-1}"]
    ka = dbg_dump("a", bigA, [128, KC * N], BF16, "a_done")
    if ka:
        FREE.setdefault("xb0", []).append(ka)

    ln_stats("ln1", "bigA", src_key_fn=lambda k: f"oev{k}")
    dbg_dump("mu1", rowA[:, :], [1, N], F32, "ln1_rows")
    dbg_dump("r1", rowB[:, :], [1, N], F32, "ln1_rows")
    ln_bcasts("ln1")

    def rcfn(e):
        last = None
        for nt in range(NT):
            e.tensor_copy(mubf[:, nt * NTW:(nt + 1) * NTW], pb[nt][:, 0:NTW])
            last = e.tensor_copy(rbf[:, nt * NTW:(nt + 1) * NTW], pb[4 + nt][:, 0:NTW])
        return last
    emit("vector", ["ln1_bc"], rcfn, s_dve, 1, key="ln1_rc")
    for j in range(8):
        consumed(f"pb{j}", "ln1_rc")
    tbv = t32a[:, :].bitcast(BF16)[:, 0:NTW]
    for k in range(KC):
        pp = k % 2
        emit("gpsimd", free_waits(f"xb{pp}"),
             lambda e, k=k, pp=pp: e.dma_start(out=xb[pp][:, :], in_=xT[k]),
             sxg[pp], 16, key=f"x1_ld{k}")

        def x1fn(e, k=k, pp=pp):
            last = None
            for nt in range(NT):
                e.tensor_tensor(tbv, bigA[:, sl(k * N, nt)],
                                mubf[:, nt * NTW:(nt + 1) * NTW], ALU.subtract)
                e.tensor_tensor(tbv, tbv, rbf[:, nt * NTW:(nt + 1) * NTW], ALU.mult)
                last = e.tensor_tensor(bigB[:, sl(k * N, nt)], tbv,
                                       xb[pp][:, nt * NTW:(nt + 1) * NTW], ALU.add)
            return last
        emit("vector", [f"x1_ld{k}", "ln1_rc", f"pre_{H-1}_{NT-1}" if k == 0 else None],
             x1fn, s_dve, 1, key=f"x1_k{k}")
        consumed(f"xb{pp}", f"x1_k{k}")
        emit("scalar", [f"x1_k{k}"],
             lambda e, k=k: e.dma_start(out=x1_sp[k], in_=bigB[:, k * N:(k + 1) * N]),
             spx, 16, key=f"x1sp{k}")
    EV["x1_done"] = EV[f"x1_k{KC-1}"]
    EV["x1_spilled"] = EV[f"x1sp{KC-1}"]

    ln_stats("ln2", "bigB", src_key_fn=lambda k: f"x1_k{k}")
    ln_bcasts("ln2")
    for k in range(KC):
        def x2fn(e, k=k):
            last = None
            for nt in range(NT):
                it = k * NT + nt
                e.tensor_tensor(t32a[:, 0:NTW], bigB[:, sl(k * N, nt)], pb[nt][:, 0:NTW],
                                ALU.subtract)
                e.tensor_tensor(bigA_h[:, sl(k * N, nt)], t32a[:, 0:NTW],
                                pb[4 + nt][:, 0:NTW], ALU.mult)
                last = e.reduce_max(xam[:, it:it + 1], bigA_h[:, sl(k * N, nt)], axis=AX.X,
                                    apply_absolute_value=True)
            return last
        emit("vector", ["ln2_bc"], x2fn, s_dve, 1, key=f"x2d_k{k}")
    EV["x2d_done"] = EV[f"x2d_k{KC-1}"]
    for j in range(8):
        consumed(f"pb{j}", f"x2d_k{KC-1}")

    def amred(e):
        e.reduce_max(amaxf[2][:, :], xam[:, 0:KC * NT], axis=AX.X)
        e.reduce_max(amaxf[0][:, :], wfam[:, :], axis=AX.X)
        return e.reduce_max(amaxf[1][:, :], wpam[:, :], axis=AX.X)
    emit("vector", ["x2d_done", "prepass"], amred, s_dve, 1, key="amred")

    def tr3(e):
        e.transpose(pb[0][0:1, 0:128], amaxf[2][:, :], idf[:, :])
        e.transpose(pb[1][0:1, 0:128], amaxf[0][:, :], idf[:, :])
        return e.transpose(pb[2][0:1, 0:128], amaxf[1][:, :], idf[:, :])
    emit("tensor", ["amred", "consts"] + free_waits("pb0")
         + free_waits("pb1") + free_waits("pb2"), tr3, s_te, 1, key="amx_tr")

    def amx(e):
        e.reduce_max(sc[0:1, S_XMAXL:S_XMAXL + 1], pb[0][0:1, 0:128], axis=AX.X)
        e.reduce_max(sc[0:1, S_WFMAX:S_WFMAX + 1], pb[1][0:1, 0:128], axis=AX.X)
        return e.reduce_max(sc[0:1, S_WPMAX:S_WPMAX + 1], pb[2][0:1, 0:128], axis=AX.X)
    emit("vector", ["amx_tr"], amx, s_dve, 1, key="amx_sc")
    consumed("pb0", "amx_sc")
    consumed("pb1", "amx_sc")
    consumed("pb2", "amx_sc")

    emit("gpsimd", ["amx_sc"], lambda e: e.dma_start(out=cc_in1.ap(), in_=sc[0:1, 0:16]),
         s_gld, 16, key="cc1_out")
    emit("gpsimd", ["cc1_out"],
         lambda e: e.collective_compute("AllReduce", ALU.max,
                                        replica_groups=[list(range(n_cores))],
                                        ins=[cc_in1.ap().opt()], outs=[cc_out1.ap().opt()]),
         s_cc, 1, key="cc1")
    emit("gpsimd", ["cc1"],
         lambda e: e.dma_start(out=sc[0:1, S_XMAXG:S_XMAXG + 1],
                               in_=cc_out1.ap()[0:1, S_XMAXL:S_XMAXL + 1]),
         s_gld, 16, key="cc1_back0")
    emit("gpsimd", ["cc1_back0"],
         lambda e: e.dma_start(out=sc[0:1, S_WFMAX:S_WPMAX + 1],
                               in_=cc_out1.ap()[0:1, S_WFMAX:S_WPMAX + 1]),
         s_gld, 16, key="cc1_back")

    def sc1(e):
        e.tensor_scalar(sc[0:1, S_SX:S_SX + 1], sc[0:1, S_XMAXG:S_XMAXG + 1],
                        1.0 / FP8_MAX, 1e-6, ALU.mult, ALU.max)
        e.tensor_scalar(sc[0:1, S_SW:S_SW + 1], sc[0:1, S_WFMAX:S_WFMAX + 1],
                        1.0 / FP8_MAX, 1e-6, ALU.mult, ALU.max)
        e.tensor_scalar(sc[0:1, S_SP:S_SP + 1], sc[0:1, S_WPMAX:S_WPMAX + 1],
                        1.0 / FP8_MAX, 1e-6, ALU.mult, ALU.max)
        e.reciprocal(sc[0:1, 0:1], sc[0:1, S_SX:S_SX + 1])
        e.reciprocal(sc[0:1, 1:2], sc[0:1, S_SW:S_SW + 1])
        e.reciprocal(sc[0:1, 2:3], sc[0:1, S_SP:S_SP + 1])
        e.scalar_tensor_tensor(sc[0:1, S_DQ1:S_DQ1 + 1], sc[0:1, S_SX:S_SX + 1], 4.0,
                               sc[0:1, S_SW:S_SW + 1], ALU.mult, ALU.mult)
        e.tensor_scalar_mul(sc[0:1, S_RSX2:S_RSX2 + 1], sc[0:1, 0:1], 0.5)
        e.tensor_scalar_mul(sc[0:1, S_RSW2:S_RSW2 + 1], sc[0:1, 1:2], 0.5)
        return e.tensor_scalar_mul(sc[0:1, S_RSP2:S_RSP2 + 1], sc[0:1, 2:3], 0.5)
    emit("vector", ["cc1_back"], sc1, s_dve, 1, key="sc1_math")
    emit("tensor", ["sc1_math"] + free_waits("pb3"),
         lambda e: e.matmul(pb[3][:, 0:NSC], onesrow[:, :], sc[0:1, 0:NSC],
                            start=True, stop=True),
         s_te, 1, key="scb1_mm")
    emit("vector", ["scb1_mm"],
         lambda e: e.tensor_copy(scb[:, :], pb[3][:, 0:NSC]),
         s_dve, 1, key="scb1")
    consumed("pb3", "scb1")

    for k in range(KC):
        if k % 2 == 0:
            def x8fn(e, k=k):
                last = None
                for nt in range(NT):
                    o = nt * (KC * NTW) + k * NTW
                    last = e.activation(x8v[:, o:o + NTW], bigA_h[:, sl(k * N, nt)],
                                        AF.Identity, bias=0.0,
                                        scale=scb[:, S_RSX2:S_RSX2 + 1])
                return last
            emit("scalar", ["scb1", "x1_spilled"], x8fn, s_act, 1, key=f"x8_k{k}")
        else:
            def x8fn(e, k=k):
                last = None
                for nt in range(NT):
                    o = nt * (KC * NTW) + k * NTW
                    last = e.tensor_scalar_mul(x8v[:, o:o + NTW], bigA_h[:, sl(k * N, nt)],
                                               scb[:, S_RSX2:S_RSX2 + 1])
                return last
            emit("vector", ["scb1", "x1_spilled"], x8fn, s_dve, 1, key=f"x8_k{k}")
    EV["x8_done"] = EV[f"x8_k{KC-1}"]

    def habs_emit(mm):
        pq = mm % 2

        def habs(e, mm=mm, pq=pq):
            return e.reduce_max(ham[:, mm:mm + 1], hbuf[pq][:, :], axis=AX.X,
                                apply_absolute_value=True)
        emit("vector", [f"hev{mm}"], habs, s_dve, 1, key=f"hmx{mm}")
        consumed_add(f"hbuf{pq}", f"hmx{mm}")

    for m in range(FC):
        pp = m % 2
        emit("sync", free_waits(f"wfbuf{pp}") + ([f"omm{MCH-1}"] if m < 2 else []),
             lambda e, m=m, pp=pp: e.dma_start(out=wfbuf[pp], in_=wfcT[m]),
             swf[pp], 16, key=f"wf_ld{m}")
        emit("vector", [f"wf_ld{m}", "scb1"] + free_waits(f"wf8b{pp}"),
             lambda e, pp=pp: e.tensor_scalar_mul(wf8b[pp], wfbuf[pp],
                                                  scb[:, S_RSW2:S_RSW2 + 1]),
             s_dve, 1, key=f"wf8_{m}")
        consumed(f"wfbuf{pp}", f"wf8_{m}")
        if m >= 2:
            habs_emit(m - 2)
        base = 0 if pp == 0 else 4

        def hmm_pair(e, c, pp, base):
            wap = wf8b[pp][:, 256 * c:256 * (c + 1)].rearrange('p (a b) -> p a b', a=2)
            last = None
            for nt in range(NT):
                o = nt * (KC * NTW) + 2 * c * NTW
                xap = x8v[:, o:o + 2 * NTW].rearrange('p (a b) -> p a b', a=2)
                last = e.matmul(pb[base + nt][:, 0:NTW], wap, xap,
                                start=(c == 0), stop=(c == KC // 2 - 1),
                                perf_mode=PM_DR)
            return last

        if m < 2:
            pbfree = []
            for nt in range(NT):
                pbfree += free_waits(f"pb{base + nt}")
            for c in range(KC // 2):
                tw = [f"wf8_{m}", f"x8_k{2*c}", f"x8_k{2*c+1}"] + (pbfree if c == 0 else [])
                emit("tensor", tw,
                     lambda e, c=c, pp=pp, base=base: hmm_pair(e, c, pp, base),
                     s_te, 1, key=f"hmm{m}_c{c}")
            EV[f"hmm{m}"] = EV[f"hmm{m}_c{KC//2-1}"]
        else:
            tw = [f"wf8_{m}", "x8_done"]
            for nt in range(NT):
                tw += free_waits(f"pb{base + nt}")

            def hmm(e, pp=pp, base=base):
                last = None
                for c in range(KC // 2):
                    last = hmm_pair(e, c, pp, base)
                return last
            emit("tensor", tw, hmm, s_te, 1, key=f"hmm{m}")
        consumed(f"wf8b{pp}", f"hmm{m}")

        def hev(e, pp=pp, base=base):
            for nt in range(NT):
                e.activation(hbuf[pp][:, nt * NTW:(nt + 1) * NTW],
                             pb[base + nt][:, 0:NTW], AF.Relu,
                             bias=0.0, scale=scb[:, S_DQ1:S_DQ1 + 1])
            return e.activation(hbuf[pp][:, :], hbuf[pp][:, :], AF.Square)
        emit("scalar", [f"hmm{m}"] + free_waits(f"hbuf{pp}"), hev, s_act, 1, key=f"hev{m}")
        for nt in range(NT):
            consumed(f"pb{base + nt}", f"hev{m}")

        emit("scalar", [f"hev{m}"],
             lambda e, m=m, pp=pp: e.dma_start(out=h_sp[m], in_=hbuf[pp]),
             sph[pp], 16, key=f"hsp{m}")
        consumed(f"hbuf{pp}", f"hsp{m}")
        if m == FC - 1:
            habs_emit(FC - 2)
            habs_emit(FC - 1)
    EV["hmax_done"] = EV[f"hmx{FC-1}"]
    dbg_dump("amax4", ham[:, :], [128, 64], F32, "hmax_done")
    dbg_dump("wfam", wfam[:, :], [128, 128], F32, "prepass")
    EV["mm1_sp0"] = EV[f"hsp{FC-2}"]
    EV["mm1_sp1"] = EV[f"hsp{FC-1}"]
    EV["mm1_mm_done"] = EV[f"hmm{FC-1}"]

    emit("vector", ["hmax_done"],
         lambda e: e.reduce_max(amaxf[3][:, :], ham[:, 0:FC], axis=AX.X),
         s_dve, 1, key="hamred")
    emit("tensor", ["hamred"] + free_waits("pb0"),
         lambda e: e.transpose(pb[0][0:1, 0:128], amaxf[3][:, :], idf[:, :]),
         s_te, 1, key="hmx_tr")
    emit("vector", ["hmx_tr"],
         lambda e: e.reduce_max(sc[0:1, S_HMAXL:S_HMAXL + 1], pb[0][0:1, 0:128], axis=AX.X),
         s_dve, 1, key="hmx_sc")
    consumed("pb0", "hmx_sc")
    emit("gpsimd", ["hmx_sc"], lambda e: e.dma_start(out=cc_in2.ap(), in_=sc[0:1, 0:16]),
         s_gld, 16, key="cc2_out")
    emit("gpsimd", ["cc2_out"],
         lambda e: e.collective_compute("AllReduce", ALU.max,
                                        replica_groups=[list(range(n_cores))],
                                        ins=[cc_in2.ap().opt()], outs=[cc_out2.ap().opt()]),
         s_cc, 1, key="cc2")
    emit("gpsimd", ["cc2"],
         lambda e: e.dma_start(out=sc[0:1, S_HMAXG:S_HMAXG + 1],
                               in_=cc_out2.ap()[0:1, S_HMAXL:S_HMAXL + 1]),
         s_gld, 16, key="cc2_back")

    def sc2(e):
        e.tensor_scalar(sc[0:1, S_SH:S_SH + 1], sc[0:1, S_HMAXG:S_HMAXG + 1],
                        1.0 / FP8_MAX, 1e-6, ALU.mult, ALU.max)
        e.tensor_copy(vsink[0:1, :], sc[0:1, S_SP:S_SP + 1])
        e.tensor_copy(vsink[0:1, :], sc[0:1, S_SP:S_SP + 1])
        e.reciprocal(sc[0:1, 1:2], sc[0:1, S_SH:S_SH + 1])
        e.scalar_tensor_tensor(sc[0:1, S_DQ2:S_DQ2 + 1], sc[0:1, S_SH:S_SH + 1], 4.0,
                               sc[0:1, S_SP:S_SP + 1], ALU.mult, ALU.mult)
        e.tensor_copy(vsink[0:1, :], sc[0:1, S_SP:S_SP + 1])
        e.tensor_copy(vsink[0:1, :], sc[0:1, S_SP:S_SP + 1])
        return e.tensor_scalar_mul(sc[0:1, S_SQS:S_SQS + 1], sc[0:1, 1:2], 0.5)
    emit("vector", ["cc2_back"], sc2, s_dve, 1, key="sc2_math")
    emit("tensor", ["sc2_math"] + free_waits("pb0"),
         lambda e: e.matmul(pb[0][:, 0:NSC], onesrow[:, :], sc[0:1, 0:NSC],
                            start=True, stop=True),
         s_te, 1, key="scb2_mm")
    emit("vector", ["scb2_mm"],
         lambda e: e.tensor_copy(scb[:, :], pb[0][:, 0:NSC]),
         s_dve, 1, key="scb2")
    consumed("pb0", "scb2")

    xh = [xbh[j // 2][:, (j % 2) * (N // 2):(j % 2 + 1) * (N // 2)] for j in range(4)]
    HW2 = N // 2
    for j, srcres in ((0, "xb0"), (2, "xb1")):
        w = free_waits(srcres)
        FREE[f"xh{j}"] = list(w)
        FREE[f"xh{j + 1}"] = list(w)
    for i in range(2 * FC):
        m, half = divmod(i, 2)
        j = i % 4
        emit("sync" if i % 2 == 0 else "gpsimd",
             [f"hsp{m}"] + free_waits(f"xh{j}"),
             lambda e, m=m, half=half, j=j: e.dma_start(
                 out=xh[j], in_=h_sp[m][:, half * HW2:(half + 1) * HW2]),
             sxb[j // 2], 16, key=f"hrd{i}")
        if i % 2 == 0:
            def h8fn(e, m=m, half=half, j=j):
                last = None
                for w in range(2):
                    o = (half * 2 + w) * (FC * NTW) + m * NTW
                    last = e.activation(h8v[:, o:o + NTW],
                                        xh[j][:, w * NTW:(w + 1) * NTW],
                                        AF.Identity, bias=0.0,
                                        scale=scb[:, S_SQS:S_SQS + 1])
                return last
            emit("scalar", [f"hrd{i}", "scb2"], h8fn, s_act, 1, key=f"h8_{i}")
        else:
            def h8fn(e, m=m, half=half, j=j):
                last = None
                for w in range(2):
                    o = (half * 2 + w) * (FC * NTW) + m * NTW
                    last = e.tensor_scalar_mul(h8v[:, o:o + NTW],
                                               xh[j][:, w * NTW:(w + 1) * NTW],
                                               scb[:, S_SQS:S_SQS + 1])
                return last
            emit("vector", [f"hrd{i}", "scb2"], h8fn, s_dve, 1, key=f"h8_{i}")
        consumed(f"xh{j}", f"h8_{i}")
    EV["h8_done"] = EV[f"h8_{2 * FC - 1}"]
    FREE["xb0"] = [f"h8_{2 * FC - 4}", f"h8_{2 * FC - 3}"]
    FREE["xb1"] = [f"h8_{2 * FC - 2}", f"h8_{2 * FC - 1}"]

    subw2 = min(2048, FC * 128)
    nsub = (FC * 128) // subw2
    for m in range(MCH):
        mp = m % 2
        for j in range(nsub):
            i = m * nsub + j
            ip = i % 2
            emit("sync", free_waits(f"wpbuf{ip}") + ([f"wf8_{FC-1}"] if i < 2 else []),
                 lambda e, m=m, j=j, ip=ip: e.dma_start(out=wpbuf[ip][:, 0:subw2],
                                                        in_=wprT[m][:, j * subw2:(j + 1) * subw2]),
                 swp[ip], 16, key=f"wp_ld{i}")
            gw = ["mm1_mm_done", "mm1_sp0", "mm1_sp1"] if i < 2 * nsub else []
            emit("vector", [f"wp_ld{i}", "scb1"] + gw
                 + (free_waits(f"wp8b{mp}") if j == 0 else []),
                 lambda e, j=j, ip=ip, mp=mp: e.tensor_scalar_mul(
                     wp8b[mp][:, j * subw2:(j + 1) * subw2], wpbuf[ip][:, 0:subw2],
                     scb[:, S_RSP2:S_RSP2 + 1]),
                 s_dve, 1, key=f"wp8_{i}")
            consumed(f"wpbuf{ip}", f"wp8_{i}")
        base = 0 if mp == 0 else 4

        def o2mm_pair(e, c, mp, base):
            wap = wp8b[mp][:, 256 * c:256 * (c + 1)].rearrange('p (a b) -> p a b', a=2)
            last = None
            for nt in range(NT):
                o = nt * (FC * NTW) + 2 * c * NTW
                xap = h8v[:, o:o + 2 * NTW].rearrange('p (a b) -> p a b', a=2)
                last = e.matmul(pb[base + nt][:, 0:NTW], wap, xap,
                                start=(c == 0), stop=(c == FC // 2 - 1),
                                perf_mode=PM_DR)
            return last

        if m < 2:
            pbfree = []
            for nt in range(NT):
                pbfree += free_waits(f"pb{base + nt}")
            for c in range(FC // 2):
                tw = [f"wp8_{m * nsub + nsub - 1}",
                      f"h8_{4*c}", f"h8_{4*c+1}", f"h8_{4*c+2}", f"h8_{4*c+3}"] \
                    + (pbfree if c == 0 else [])
                emit("tensor", tw,
                     lambda e, c=c, mp=mp, base=base: o2mm_pair(e, c, mp, base),
                     s_te, 1, key=f"omm2_{m}_c{c}")
            EV[f"omm2_{m}"] = EV[f"omm2_{m}_c{FC//2-1}"]
        else:
            tw = [f"wp8_{m * nsub + nsub - 1}", "h8_done"]
            for nt in range(NT):
                tw += free_waits(f"pb{base + nt}")

            def o2mm(e, mp=mp, base=base):
                last = None
                for c in range(FC // 2):
                    last = o2mm_pair(e, c, mp, base)
                return last
            emit("tensor", tw, o2mm, s_te, 1, key=f"omm2_{m}")
        consumed(f"wp8b{mp}", f"omm2_{m}")

        def o2ev(e, mp=mp, base=base):
            last = None
            for nt in range(NT):
                last = e.activation(obuf[mp][:, nt * NTW:(nt + 1) * NTW],
                                    pb[base + nt][:, 0:NTW], AF.Identity,
                                    bias=0.0, scale=scb[:, S_DQ2:S_DQ2 + 1])
            return last
        emit("scalar", [f"omm2_{m}"] + free_waits(f"obuf{mp}"), o2ev, s_act, 1, key=f"oev2_{m}")
        for nt in range(NT):
            consumed(f"pb{base + nt}", f"oev2_{m}")
        emit("scalar", [f"oev2_{m}"],
             lambda e, m=m, mp=mp: e.dma_start(out=o_sp[m], in_=obuf[mp]),
             spo[mp], 16, key=f"osp{m}")
        consumed(f"obuf{mp}", f"osp{m}")
    EV["mm2_sp0"] = EV[f"osp{MCH-2}"]
    EV["mm2_sp1"] = EV[f"osp{MCH-1}"]

    ln_stats("ln3", "osp")
    ln_bcasts("ln3")

    def rcfn3(e):
        last = None
        for nt in range(NT):
            e.tensor_copy(muh[:, nt * NTW:(nt + 1) * NTW], pb[nt][:, 0:NTW])
            last = e.tensor_copy(rh[:, nt * NTW:(nt + 1) * NTW], pb[4 + nt][:, 0:NTW])
        return last
    emit("vector", ["ln3_bc"], rcfn3, s_dve, 1, key="ln3_rc")
    for j in range(8):
        consumed(f"pb{j}", "ln3_rc")
    thv = t32a[:, :].bitcast(F16)[:, 0:NTW]
    for k in range(KC):
        pp = k % 2
        emit("sync", [f"osp{k}"] + free_waits(f"xb{pp}"),
             lambda e, k=k, pp=pp: e.dma_start(out=xbh[pp], in_=o_sp[k]),
             sxb[pp], 16, key=f"ord{k}")
        emit("gpsimd", free_waits(f"x1b{pp}")
             + (["x1_spilled", f"omm2_{MCH-1}"] if k < 2 else []),
             lambda e, k=k, pp=pp: e.dma_start(out=x1b[pp], in_=x1_sp[k]),
             sx1b[pp], 16, key=f"x1rd{k}")

        def ofn(e, k=k, pp=pp):
            last = None
            for nt in range(NT):
                e.tensor_tensor(thv, xbh[pp][:, nt * NTW:(nt + 1) * NTW],
                                muh[:, nt * NTW:(nt + 1) * NTW], ALU.subtract)
                e.tensor_tensor(thv, thv, rh[:, nt * NTW:(nt + 1) * NTW], ALU.mult)
                last = e.tensor_tensor(outb[pp][:, nt * NTW:(nt + 1) * NTW], thv,
                                       x1b[pp][:, nt * NTW:(nt + 1) * NTW], ALU.add)
            return last
        emit("vector", [f"ord{k}", f"x1rd{k}", "ln3_rc"] + free_waits(f"outb{pp}"),
             ofn, s_dve, 1, key=f"out_k{k}")
        consumed(f"xb{pp}", f"out_k{k}")
        consumed(f"x1b{pp}", f"out_k{k}")
        emit("sync" if k % 2 == 0 else "gpsimd", [f"out_k{k}"],
             lambda e, k=k, pp=pp: e.dma_start(out=outT[k], in_=outb[pp]),
             sout[pp], 16, key=f"st{k}")
        consumed(f"outb{pp}", f"st{k}")

    emit("sync", [f"st{KC-2}", f"st{KC-1}"], lambda e: e.nop())
    emit("tensor", [f"st{KC-2}", f"st{KC-1}"], lambda e: e.nop())

    def replay(eng_name):
        def fn(e):
            for wl, build, semx, n in recs[eng_name]:
                for s, v in wl:
                    e.wait_ge(s, v)
                instr = build(e)
                if semx is not None:
                    instr.then_inc(semx, n)
        return fn

    with nc.Block() as B:
        B.gpsimd(replay("gpsimd"))
        B.sync(replay("sync"))
        B.tensor(replay("tensor"))
        B.scalar(replay("scalar"))
        B.vector(replay("vector"))

    es.close()
    return nc


_D, _N, _H, _F, _NCORES = 2048, 2048, 16, 8192, 8
_nc_cache = {}


def _tile_wT(w):
    d_out, d_in = w.shape
    MCH, KC = d_out // 128, d_in // 128
    t = np.asarray(w, dtype=np.float32).reshape(MCH, 128, KC, 128)
    t = t.transpose(0, 3, 2, 1)
    return np.ascontiguousarray(t.reshape(MCH, 128, KC * 128))


def kernel(L0_attn_x, L0_attn_Wq, L0_attn_bq, L0_attn_Wk, L0_attn_bk,
           L0_attn_Wv, L0_attn_bv, L0_attn_Wo, L0_attn_bo,
           L0_ffn_w_fc, L0_ffn_w_proj):
    D, N, H, F, C = _D, _N, _H, _F, _NCORES
    KC = D // 128
    TB = N // 128

    x = np.ascontiguousarray(np.asarray(L0_attn_x, dtype=np.float32))
    assert x.shape == (C, N, D)

    if "nc" not in _nc_cache:
        _nc_cache["nc"] = build_nc(D, N, H, F, C)
    nc = _nc_cache["nc"]

    shared = dict(
        wqT=_tile_wT(L0_attn_Wq), wkT=_tile_wT(L0_attn_Wk), wvT=_tile_wT(L0_attn_Wv),
        woT=_tile_wT(L0_attn_Wo), wfcT=_tile_wT(L0_ffn_w_fc), wprT=_tile_wT(L0_ffn_w_proj),
    )

    def col(b):
        return np.ascontiguousarray(np.asarray(b, dtype=np.float32).reshape(KC, 128).T)

    pos = np.arange(N, dtype=np.float32)
    decay = np.exp(-pos / np.float32(N)).astype(np.float32)
    shared.update(
        bq_c=col(L0_attn_bq), bk_c=col(L0_attn_bk), bv_c=col(L0_attn_bv),
        bo_c=col(L0_attn_bo),
        identf=np.eye(128, dtype=np.float32),
        identb=np.eye(128, dtype=ml_dtypes.bfloat16),
        decrow_d=decay.reshape(1, N),
        deccol_d=np.ascontiguousarray(decay.reshape(TB, 128).T),
    )

    FC8 = (F // 128) // C
    M8 = (D // 128) // C
    in_maps = []
    for c in range(C):
        m = dict(shared)
        m["xT"] = np.ascontiguousarray(x[c].T).reshape(KC, 128, N)
        m["wfsc"] = shared["wfcT"][c * FC8:(c + 1) * FC8]
        m["wpsc"] = shared["wprT"][c * M8:(c + 1) * M8]
        in_maps.append(m)

    res = run_bass_kernel_spmd(nc, in_maps, core_ids=list(range(C)))
    out = np.empty((C, N, D), dtype=np.float32)
    for c in range(C):
        out[c] = res.results[c]["outT"].reshape(D, N).T
    return out
